# version 07
# baseline (speedup 1.0000x reference)
import os
import numpy as np
import concourse.bass as bass
import concourse.mybir as mybir
from concourse.bass_utils import run_bass_kernel_spmd

F32 = mybir.dt.float32
BF16 = mybir.dt.bfloat16
AF = mybir.ActivationFunctionType
OP = mybir.AluOpType
AX = mybir.AxisListType

NB = 2
T = 2048
D = 1024
N16 = 16
P120 = 120
L = 32
CCH = 64
NT = 32
DS = 8
FS = 32
TCH2 = 2
NCH2 = NT // TCH2
NTK = TCH2 * 128


def build_nc(dbg=False):
    nc = bass.Bass("TRN2", target_bir_lowering=False)
    dt = nc.dram_tensor

    x_in = dt("x", [NB, T, D], F32, kind="ExternalInput").ap()
    wpT_d = dt("wpT", [D, P120], F32, kind="ExternalInput").ap()
    wpb_d = dt("wpb", [P120, 1], F32, kind="ExternalInput").ap()
    bwT_d = dt("bwT", [D, N16], F32, kind="ExternalInput").ap()
    cT_d = dt("cT", [N16, D], F32, kind="ExternalInput").ap()
    dv_d = dt("dvec", [1, D], F32, kind="ExternalInput").ap()
    wfcT_d = dt("wfcT", [D, 4 * D], F32, kind="ExternalInput").ap()
    wprT_d = dt("wprT", [4 * D, D], F32, kind="ExternalInput").ap()
    cmod_d = dt("cmod", [NB, D], F32, kind="ExternalInput").ap()
    wmod_d = dt("wmod", [D, 6 * D], F32, kind="ExternalInput").ap()
    mc_d = dt("mc", [P120, 256], F32, kind="ExternalInput").ap()
    i128b_d = dt("i128b", [128, 128], BF16, kind="ExternalInput").ap()
    i128f_d = dt("i128f", [128, 128], F32, kind="ExternalInput").ap()
    i16_d = dt("i16", [N16, N16], F32, kind="ExternalInput").ap()
    itile_d = dt("itile", [128, 256], F32, kind="ExternalInput").ap()
    knl_d = dt("knl", [4, 128], F32, kind="ExternalInput").ap()
    bsel_d = dt("bsel", [2, 128], F32, kind="ExternalInput").ap()
    bselb_d = dt("bselb", [2, 128], BF16, kind="ExternalInput").ap()
    esel_d = dt("esel", [2, 32], F32, kind="ExternalInput").ap()
    u0_d = dt("u0", [128, NT * N16], F32, kind="ExternalInput").ap()
    cvals_d = dt("cvals", [128, 2], F32, kind="ExternalInput").ap()

    out_d = dt("out", [NB, T, D], F32, kind="ExternalOutput").ap()
    wprB_d = dt("wprB", [4 * D, D], BF16, kind="Internal").ap()

    if dbg:
        dbg_w = dt("dbg_w", [128, NT * 256], F32, kind="ExternalOutput").ap()
        dbg_hall = dt("dbg_hall", [128, NT * N16], F32, kind="ExternalOutput").ap()
        dbg_bxs = dt("dbg_bxs", [128, NT * N16], F32, kind="ExternalOutput").ap()
        dbg_sc = dt("dbg_sc", [128, NT], F32, kind="ExternalOutput").ap()
        dbg_a = dt("dbg_a", [128, 4 * 256], F32, kind="ExternalOutput").ap()
        dbg_v = dt("dbg_v", [P120, 512], F32, kind="ExternalOutput").ap()
        dbg_mod = dt("dbg_mod", [NB, 6 * D], F32, kind="ExternalOutput").ap()
        dbg_xmid = dt("dbg_xmid", [128, TCH2 * D], F32, kind="ExternalOutput").ap()
        dbg_bc = dt("dbg_bc", [128, 2 * D], F32, kind="ExternalOutput").ap()
        dbg_st = dt("dbg_st", [128, 4 * NT], F32, kind="ExternalOutput").ap()
        dbg_hdm = dt("dbg_hdm", [128, DS * 512], BF16, kind="ExternalOutput").ap()
        dbg_xn = dt("dbg_xn", [128, D], BF16, kind="ExternalOutput").ap()
        dbg_htk = dt("dbg_htk", [128, D], BF16, kind="ExternalOutput").ap()
        dbg_st0 = dt("dbg_st0", [128, 8], F32, kind="ExternalOutput").ap()
        dbg_pi = dt("dbg_pi", [128, 72], F32, kind="ExternalOutput").ap()
        dbg_sc2 = dt("dbg_sc2", [128, 256 + 16 + 16], F32, kind="ExternalOutput").ap()
        dbg_hb = dt("dbg_hb", [N16, 3 * 128], F32, kind="ExternalOutput").ap()

    xr = x_in.rearrange("b (c s) d -> s (b c) d", s=L)
    orr = out_d.rearrange("b (c s) d -> s (b c) d", s=L)

    cnt = {}

    def bump(instr, sem, n=1):
        instr.then_inc(sem, n)
        cnt[sem.name] = cnt.get(sem.name, 0) + n
        return cnt[sem.name]

    def val(sem):
        return cnt.get(sem.name, 0)

    def chain(e, sem):
        state = {'first': True}

        def emit(mk, snap=False):
            if not state['first']:
                e.drain()
            state['first'] = False
            instr = mk()
            if snap:
                return bump(instr, sem, 1)
            return None
        return emit

    import contextlib
    est = contextlib.ExitStack()
    with est:
        def S(name, shape, dtp):
            return est.enter_context(nc.sbuf_tensor(name, shape, dtp))

        def SEM(name):
            return est.enter_context(nc.semaphore(name))

        hallT = S("hallT", [N16, NT * 128], BF16)
        st_m = S("st_m", [128, NT], F32)
        st_r = S("st_r", [128, NT], F32)
        st_mr = S("st_mr", [128, NT], F32)
        st_s = S("st_s", [128, NT], F32)
        st_t = S("st_t", [128, NT], F32)
        st2_m = S("st2_m", [128, NT], F32)
        st2_r = S("st2_r", [128, NT], F32)
        st2_mr = S("st2_mr", [128, NT], F32)
        mc_s = S("mc_s", [P120, 256], F32)
        i128 = S("i128", [128, 128], BF16)
        if32 = S("if32", [128, 128], F32)
        i16s = S("i16s", [N16, N16], F32)
        knls = S("knls", [4, 128], F32)
        bsel = S("bsel_s", [2, 128], F32)
        bselb = S("bselb_s", [2, 128], BF16)
        esel = S("esel_s", [2, 32], F32)
        itls = S("itls", [128, 256], F32)
        cvals = S("cvals_s", [128, 2], F32)
        wpb16 = S("wpb16", [128, DS * P120], BF16)
        bwb16 = S("bwb16", [128, DS * N16], BF16)
        wpbT = S("wpbT", [P120, 1], F32)
        ctg = S("ctg", [N16, NB * D], BF16)
        gdca_t = S("gdca_t", [128, D], F32)
        gdsa = S("gdsa", [NB, D], BF16)
        cb1_bc = S("cb1_bc", [128, D], F32)
        sb_bc = S("sb_bc", [128, D], F32)
        gbsT = S("gbsT", [128, NB * DS], F32)
        eB = contextlib.ExitStack()
        wtok = eB.enter_context(nc.sbuf_tensor("wtok", [128, NT * 256], F32, side="right"))
        bxs = eB.enter_context(nc.sbuf_tensor("bxs", [128, NT * N16], F32, side="right"))
        hall = eB.enter_context(nc.sbuf_tensor("hall", [128, NT * N16], F32, side="right"))
        u0s = eB.enter_context(nc.sbuf_tensor("u0s", [128, NT * N16], F32, side="right"))
        ca1_bc = eB.enter_context(nc.sbuf_tensor("ca1_bc", [128, D], F32, side="right"))
        sa_bc = eB.enter_context(nc.sbuf_tensor("sa_bc", [128, D], F32, side="right"))
        sq = SEM("sq")
        gq = SEM("gq")
        tv = SEM("tv")
        vv = SEM("vv")
        ss = SEM("ss")
        gg = SEM("gg")
        so = SEM("so")
        gs = SEM("gs")
        bar = SEM("bar")
        blk = est.enter_context(nc.Block())
        nbar = [0]

        def barrier():
            nbar[0] += 5
            tgt = nbar[0]
            for name in ("vector", "scalar", "tensor", "gpsimd", "sync"):
                def _f(engine, tgt=tgt):
                    engine.drain().then_inc(bar, 1)
                    engine.wait_ge(bar, tgt)
                getattr(blk, name)(_f)

        wpr_done = [0]
        wfc_done = [0]
        e0 = contextlib.ExitStack()
        with e0:
            def S0(name, shape, dtp):
                return e0.enter_context(nc.sbuf_tensor(name, shape, dtp))

            e0buf = S0("e0buf", [128, 2 * 1536], F32)
            modsb = S0("modsb", [NB, 6 * D], F32)
            silu_c = S0("silu_c", [NB, D], F32)
            scT = S0("scT", [128, NB * DS], F32)
            mt0 = S0("mt0", [NB, D], F32)
            mt1 = S0("mt1", [NB, D], F32)
            mt2 = S0("mt2", [NB, D], F32)
            dsb = S0("dsb", [1, D], F32)
            wpraw = S0("wpraw", [128, DS * P120], F32)
            bwraw = S0("bwraw", [128, DS * N16], F32)
            ctraw = S0("ctraw", [N16, D], F32)
            psm0 = e0.enter_context(nc.psum_tensor("psm0", [NB, 1536], F32))
            pse0 = e0.enter_context(nc.psum_tensor("pse0", [128, 1024], F32))
            pse1 = e0.enter_context(nc.psum_tensor("pse1", [N16, D], F32))
            @blk.sync
            def _(e):
                for t_, s_ in ((mc_s, mc_d), (i128, i128b_d), (if32, i128f_d),
                               (i16s, i16_d), (knls, knl_d), (bsel, bsel_d), (bselb, bselb_d), (esel, esel_d),
                               (itls, itile_d), (u0s, u0_d), (cvals, cvals_d), (silu_c, cmod_d),
                               (ctraw, cT_d), (dsb, dv_d), (wpbT, wpb_d)):
                    bump(e.dma_start(out=t_[:], in_=s_), sq, 16)
                bump(e.dma_start(
                    out=wpraw[:].rearrange("p (s k) -> p s k", s=DS),
                    in_=wpT_d.rearrange("(s p) k -> p s k", p=128)), sq, 16)
                bump(e.dma_start(
                    out=bwraw[:].rearrange("p (s k) -> p s k", s=DS),
                    in_=bwT_d.rearrange("(s p) k -> p s k", p=128)), sq, 16)
            ld_small = val(sq)

            @blk.gpsimd
            def _(e):
                for s in range(2):
                    bump(e.dma_start(
                        out=wprB_d[s * 2048:(s + 1) * 2048, :],
                        in_=wprT_d[s * 2048:(s + 1) * 2048, :]), gq, 16)
            wpr_done[0] = val(gq)

            @blk.scalar
            def _(e):
                e.wait_ge(sq, ld_small)
                bump(e.activation(silu_c[:], silu_c[:], AF.Silu), ss, 1)
            silu_done = val(ss)

            @blk.vector
            def _(e):
                e.wait_ge(sq, ld_small)
                bump(e.tensor_copy(wpb16[:], wpraw[:]), vv, 1)
                bump(e.tensor_copy(bwb16[:], bwraw[:]), vv, 1)
            cast_done = val(vv)

            @blk.tensor
            def _(e):
                e.wait_ge(ss, silu_done)
                for s in range(DS):
                    bump(e.matmul(pse0[:, 2 * s:2 * s + 2],
                                  silu_c[:, s * 128:(s + 1) * 128],
                                  i16s[0:2, 0:2], start=True, stop=True), tv, 1)
            scT_mm = val(tv)

            @blk.vector
            def _(e):
                e.wait_ge(tv, scT_mm)
                bump(e.tensor_copy(scT[:], pse0[:, 0:2 * DS]), vv, 1)
            scT_done = val(vv)

            @blk.sync
            def _(e):
                for h in range(4):
                    for s in range(DS):
                        i = h * DS + s
                        slot = i % 2
                        if i >= 2:
                            e.wait_ge(tv, scT_mm + (i - 1) * 3)
                        bump(e.dma_start(
                            out=e0buf[:, slot * 1536:(slot + 1) * 1536],
                            in_=wmod_d[s * 128:(s + 1) * 128,
                                       h * 1536:(h + 1) * 1536]), sq, 16)
            wmod_ld = val(sq)

            mod_ev = [0, 0, 0, 0]
            for h in range(4):
                @blk.tensor
                def _(e, h=h):
                    e.wait_ge(vv, scT_done)
                    if h >= 1:
                        e.wait_ge(vv, mod_ev[h - 1])
                    for s in range(DS):
                        i = h * DS + s
                        slot = i % 2
                        e.wait_ge(sq, wmod_ld - (31 - i) * 16)
                        for n in range(3):
                            bump(e.matmul(
                                psm0[:, n * 512:(n + 1) * 512],
                                scT[:, 2 * s:2 * s + 2],
                                e0buf[:, slot * 1536 + n * 512:
                                      slot * 1536 + (n + 1) * 512],
                                start=(s == 0), stop=(s == DS - 1)), tv, 1)
                mmh = val(tv)

                @blk.vector
                def _(e, h=h, mmh=mmh):
                    e.wait_ge(tv, mmh)
                    mod_ev[h] = bump(e.tensor_copy(
                        modsb[:, h * 1536:(h + 1) * 1536], psm0[:, 0:1536]), vv, 1)

            SA = slice(0, D); CA = slice(D, 2 * D); GA = slice(2 * D, 3 * D)
            SB_ = slice(3 * D, 4 * D); CB = slice(4 * D, 5 * D)
            GB = slice(5 * D, 6 * D)

            @blk.vector
            def _(e):
                e.wait_ge(vv, mod_ev[3])
                c = chain(e, vv)
                for gsl in (GA, GB):
                    c(lambda gsl=gsl: e.tensor_single_scalar(
                        mt0[:], modsb[:, gsl], 0.0, op=OP.is_ge))
                    c(lambda: e.tensor_scalar(mt0[:], mt0[:], 2.0, -1.0,
                                              op0=OP.mult, op1=OP.add))
                    c(lambda gsl=gsl: e.tensor_mul(mt1[:], modsb[:, gsl],
                                                   mt0[:]))
                    c(lambda: e.tensor_scalar_max(mt1[:], mt1[:], 0.25))
                    c(lambda gsl=gsl: e.tensor_mul(modsb[:, gsl], mt1[:],
                                                   mt0[:]))
                c(lambda: e.tensor_scalar_add(modsb[:, CA], modsb[:, CA], 1.0))
                c(lambda: e.tensor_scalar_add(modsb[:, CB], modsb[:, CB], 1.0),
                  snap=True)
            clamp_done = val(vv)

            @blk.tensor
            def _(e):
                e.wait_ge(vv, clamp_done)
                for n in range(2):
                    bump(e.matmul(pse1[0:2, n * 512:(n + 1) * 512],
                                  knls[0:1, 0:2], dsb[:, n * 512:(n + 1) * 512],
                                  start=True, stop=True), tv, 1)
            dexp_mm = val(tv)

            @blk.vector
            def _(e):
                e.wait_ge(tv, dexp_mm)
                c = chain(e, vv)
                c(lambda: e.tensor_mul(mt2[:], modsb[:, GA], pse1[0:2, :]))
                c(lambda: e.tensor_mul(gdsa[:], mt2[:], modsb[:, SA]))
                c(lambda: e.tensor_mul(mt1[:], mt2[:], modsb[:, CA]))
                c(lambda: e.tensor_scalar_mul(mt0[:], modsb[:, GB], 0.2),
                  snap=True)
            drows_done = val(vv)

            @blk.tensor
            def _(e):
                e.wait_ge(vv, drows_done)
                for s in range(DS):
                    bump(e.matmul(pse0[:, 2 * s:2 * s + 2],
                                  mt0[:, s * 128:(s + 1) * 128],
                                  i16s[0:2, 0:2], start=True, stop=True), tv, 1)
            gbs_mm = val(tv)

            @blk.vector
            def _(e):
                e.wait_ge(tv, gbs_mm)
                bump(e.tensor_copy(gbsT[:], pse0[:, 0:2 * DS]), vv, 1)
            gbs_done = val(vv)

            ctg_done = [0]
            for b in range(NB):
                @blk.tensor
                def _(e, b=b):
                    e.wait_ge(vv, gbs_done if b == 0 else ctg_done[0])
                    for n in range(2):
                        bump(e.matmul(pse1[:, n * 512:(n + 1) * 512],
                                      esel[:, b * N16:(b + 1) * N16],
                                      modsb[0:2, GA][:, n * 512:(n + 1) * 512],
                                      start=True, stop=True), tv, 1)
                cmm = val(tv)

                @blk.vector
                def _(e, b=b, cmm=cmm):
                    e.wait_ge(tv, cmm)
                    ctg_done[0] = bump(e.tensor_mul(
                        ctg[:, b * D:(b + 1) * D], ctraw[:], pse1[:]), vv, 1)

            bspec = [(ca1_bc, CA, None), (sa_bc, SA, None), (cb1_bc, CB, None),
                     (sb_bc, SB_, None), (gdca_t, None, mt1)]
            bct_prev = [ctg_done[0]]
            for ti, (dst, rsl, alt) in enumerate(bspec):
                @blk.tensor
                def _(e, rsl=rsl, alt=alt):
                    e.wait_ge(vv, bct_prev[0])
                    for n in range(2):
                        rows = (modsb[0:2, rsl] if alt is None
                                else alt[0:2, :])[:, n * 512:(n + 1) * 512]
                        bump(e.matmul(pse0[:, n * 512:(n + 1) * 512],
                                      bsel[:], rows, start=True, stop=True),
                             tv, 1)
                bmm = val(tv)

                @blk.vector
                def _(e, dst=dst, bmm=bmm):
                    e.wait_ge(tv, bmm)
                    bct_prev[0] = bump(e.tensor_copy(dst[:], pse0[:]), vv, 1)

            if dbg:
                @blk.gpsimd
                def _(e):
                    e.wait_ge(vv, bct_prev[0])
                    bump(e.dma_start(out=dbg_mod, in_=modsb[:]), gg, 16)
                    bump(e.dma_start(out=dbg_bc[:, 0:D], in_=ca1_bc[:]), gg, 16)
                    bump(e.dma_start(out=dbg_bc[:, D:2 * D], in_=sa_bc[:]), gg, 16)

        barrier()

        e1 = contextlib.ExitStack()
        with e1:
            def S1(name, shape, dtp):
                return e1.enter_context(nc.sbuf_tensor(name, shape, dtp))

            xbuf = S1("xbuf", [128, 2 * D], F32)
            sqb = S1("sqb", [128, D], F32)
            xnb = S1("xnb", [128, D], BF16)
            htk = S1("htk", [128, D], BF16)
            hdm = S1("hdm", [128, DS * 512], BF16)
            vsb = S1("vsb", [P120, 512], F32)
            bxd = S1("bxd", [N16, 512], F32)
            atok = S1("atok", [128, 4 * 256], F32)
            a2b = S1("a2b", [128, 256], F32)
            a3b = S1("a3b", [128, 256], F32)
            a4b = S1("a4b", [128, 256], F32)
            wcmb = S1("wcmb", [128, 256], F32)
            wcm2 = S1("wcm2", [128, 256], F32)
            tmp4k = S1("tmp4k", [128, 4096], F32)
            piv = S1("piv", [128, 256], F32)
            pisq = S1("pisq", [128, 64], F32)
            psv = e1.enter_context(nc.psum_tensor("psv", [P120, 512], F32))
            psbx = e1.enter_context(nc.psum_tensor("psbx", [N16, 512], F32))
            psbt = e1.enter_context(nc.psum_tensor("psbt", [128, 64], F32))
            psa = e1.enter_context(nc.psum_tensor("psa", [128, 4 * 256], F32))
            pstr = e1.enter_context(nc.psum_tensor("pstr", [128, 1024], F32))
            xld = [0] * NT
            sqd = [0] * NT
            xnd = [0] * NT
            sdd = [0] * NT
            var_d = [0] * NT
            rch = [0] * NT
            hdone = [0] * NT
            httr = [0] * NT
            hev = [0] * NT
            ssq_red = [0] * NT
            piq = [0] * NT
            pisd = [0] * NT
            aev = [0] * NT
            wdn = [0] * NT
            vev = [0] * 8
            bxe = [0] * 8
            vmm = [0] * 8
            bxmm = [0] * 8
            bxtr = [0] * 8
            amm = [0] * 8
            bxs_ev = [0] * 8

            for ch in range(8):
                for tau in range(4):
                    g = 4 * ch + tau
                    sl = g % 2

                    @blk.sync
                    def _(e, g=g, sl=sl):
                        if g >= 2:
                            e.wait_ge(vv, xnd[g - 2])
                        xld[g] = bump(e.dma_start(
                            out=xbuf[:, sl * D:(sl + 1) * D], in_=xr[g]), sq, 16)

                    @blk.scalar
                    def _(e, g=g, sl=sl):
                        e.wait_ge(sq, xld[g])
                        if g >= 1:
                            e.wait_ge(vv, ssq_red[g - 1])
                        sqd[g] = bump(e.activation(
                            sqb[:],
                            xbuf[:, sl * D:(sl + 1) * D], AF.Square), ss, 1)

                    @blk.vector
                    def _(e, g=g, sl=sl):
                        xt = xbuf[:, sl * D:(sl + 1) * D]
                        e.wait_ge(sq, xld[g])
                        c = chain(e, vv)
                        c(lambda: e.reduce_sum(st_m[:, g:g + 1], xt, axis=AX.X))
                        e.wait_ge(ss, sqd[g])
                        ssq_red[g] = c(lambda: e.reduce_sum(
                            st_t[:, g:g + 1], sqb[:], axis=AX.X), snap=True)
                        c(lambda: e.tensor_scalar_mul(
                            st_m[:, g:g + 1], st_m[:, g:g + 1], 1.0 / D))
                        c(lambda: e.tensor_scalar_mul(
                            st_t[:, g:g + 1], st_t[:, g:g + 1], 1.0 / D))
                        c(lambda: e.tensor_mul(st_s[:, g:g + 1],
                                               st_m[:, g:g + 1],
                                               st_m[:, g:g + 1]))
                        c(lambda: e.tensor_sub(st_t[:, g:g + 1],
                                               st_t[:, g:g + 1],
                                               st_s[:, g:g + 1]))
                        var_d[g] = c(lambda: e.tensor_scalar_add(
                            st_t[:, g:g + 1], st_t[:, g:g + 1], 1e-5), snap=True)

                    if dbg and g == 0:
                        @blk.gpsimd
                        def _(e):
                            e.wait_ge(vv, var_d[0])
                            with nc.allow_non_contiguous_dma(reason="dbg"):
                                bump(e.dma_start(out=dbg_st0[:, 3:4],
                                                 in_=st_t[:, 0:1]), gg, 16)
                        dbg_vd = val(gg)

                    @blk.scalar
                    def _(e, g=g):
                        e.wait_ge(vv, var_d[g])
                        if dbg and g == 0:
                            e.wait_ge(gg, dbg_vd)
                        e.activation(st_r[:, g:g + 1], st_t[:, g:g + 1], AF.Ln)
                        e.drain()
                        sdd[g] = bump(e.activation(
                            st_r[:, g:g + 1], st_r[:, g:g + 1], AF.Exp,
                            scale=cvals[:, 0:1]), ss, 1)

                    if dbg and g == 0:
                        @blk.gpsimd
                        def _(e):
                            e.wait_ge(ss, sdd[0])
                            with nc.allow_non_contiguous_dma(reason="dbg"):
                                bump(e.dma_start(out=dbg_st0[:, 4:5],
                                                 in_=st_r[:, 0:1]), gg, 16)
                        dbg_sd = val(gg)

                    @blk.vector
                    def _(e, g=g):
                        e.wait_ge(ss, sdd[g])
                        if dbg and g == 0:
                            e.wait_ge(gg, dbg_sd)
                        rch[g] = bump(e.scalar_tensor_tensor(
                            st_mr[:, g:g + 1], st_m[:, g:g + 1], -1.0,
                            st_r[:, g:g + 1], op0=OP.mult, op1=OP.mult), vv, 1)

                    @blk.vector
                    def _(e, g=g, sl=sl):
                        if g >= 1:
                            e.wait_ge(tv, httr[g - 1])
                        e.wait_ge(vv, rch[g])
                        c = chain(e, vv)
                        c(lambda: e.tensor_tensor(
                            xnb[:], xbuf[:, sl * D:(sl + 1) * D],
                            st_r[:, g:g + 1].broadcast_to([128, D]),
                            op=OP.mult))
                        xnd[g] = c(lambda: e.tensor_tensor(
                            xnb[:], xnb[:],
                            st_mr[:, g:g + 1].broadcast_to([128, D]),
                            op=OP.add), snap=True)

                    @blk.vector
                    def _(e, g=g):
                        e.wait_ge(vv, xnd[g])
                        c = chain(e, vv)
                        c(lambda: e.tensor_mul(htk[:], xnb[:], ca1_bc[:]))
                        hdone[g] = c(lambda: e.tensor_add(
                            htk[:], htk[:], sa_bc[:]), snap=True)

                    if dbg and g == 0:
                        @blk.gpsimd
                        def _(e):
                            e.wait_ge(vv, hdone[0])
                            bump(e.dma_start(out=dbg_xn, in_=xnb[:]), gg, 16)
                            bump(e.dma_start(out=dbg_htk, in_=htk[:]), gg, 16)
                            with nc.allow_non_contiguous_dma(reason="dbg"):
                                bump(e.dma_start(out=dbg_st0[:, 0:1], in_=st_m[:, 0:1]), gg, 16)
                                bump(e.dma_start(out=dbg_st0[:, 1:2], in_=st_r[:, 0:1]), gg, 16)
                                bump(e.dma_start(out=dbg_st0[:, 2:3], in_=st_mr[:, 0:1]), gg, 16)
                        dbg_g0 = val(gg)

                    if dbg and g == 1:
                        @blk.vector
                        def _(e):
                            e.wait_ge(gg, dbg_g0)

                    @blk.tensor
                    def _(e, g=g):
                        e.wait_ge(vv, hdone[g])
                        if g >= 1:
                            e.wait_ge(vv, hev[g - 1])
                        for sdx in range(DS):
                            httr[g] = bump(e.matmul(
                                pstr[:, sdx * 128:(sdx + 1) * 128],
                                htk[:, sdx * 128:(sdx + 1) * 128],
                                i128[:], start=True, stop=True), tv, 1)

                    @blk.vector
                    def _(e, g=g, tau=tau, ch=ch):
                        e.wait_ge(tv, httr[g])
                        if ch > 0 and tau == 0:
                            e.wait_ge(tv, bxmm[ch - 1])
                        hev[g] = bump(e.tensor_copy(
                            hdm[:].rearrange("p (s t) -> p s t",
                                             s=DS)[:, :, tau * 128:(tau + 1) * 128],
                            pstr[:].rearrange("p (s t) -> p s t", s=DS)), vv, 1)

                @blk.tensor
                def _(e, ch=ch):
                    e.wait_ge(vv, hev[4 * ch + 3])
                    if ch > 0:
                        e.wait_ge(vv, vev[ch - 1])
                    for k in range(DS):
                        vmm[ch] = bump(e.matmul(
                            psv[:], wpb16[:, k * P120:(k + 1) * P120],
                            hdm[:, k * 512:(k + 1) * 512],
                            start=(k == 0), stop=(k == DS - 1)), tv, 1)
                    if ch > 0:
                        e.wait_ge(vv, bxe[ch - 1])
                    for k in range(DS):
                        bxmm[ch] = bump(e.matmul(
                            psbx[:], bwb16[:, k * N16:(k + 1) * N16],
                            hdm[:, k * 512:(k + 1) * 512],
                            start=(k == 0), stop=(k == DS - 1)), tv, 1)

                @blk.vector
                def _(e, ch=ch):
                    e.wait_ge(tv, vmm[ch])
                    vev[ch] = bump(e.tensor_scalar_add(vsb[:], psv[:], wpbT[:]),
                                   vv, 1)
                    e.wait_ge(tv, bxmm[ch])
                    bxe[ch] = bump(e.tensor_copy(bxd[:], psbx[:]), vv, 1)

                @blk.tensor
                def _(e, ch=ch):
                    e.wait_ge(vv, bxe[ch])
                    if ch > 0:
                        e.wait_ge(ss, bxs_ev[ch - 1])
                    for tau in range(4):
                        bxtr[ch] = bump(e.matmul(
                            psbt[:, tau * N16:(tau + 1) * N16],
                            bxd[:, tau * 128:(tau + 1) * 128], i16s[:],
                            start=True, stop=True), tv, 1)
                    e.wait_ge(vv, vev[ch])
                    if ch > 0:
                        e.wait_ge(vv, aev[4 * ch - 1])
                    for tau in range(4):
                        amm[ch] = bump(e.matmul(
                            psa[:, tau * 256:(tau + 1) * 256],
                            vsb[:, tau * 128:(tau + 1) * 128], mc_s[:],
                            start=True, stop=True), tv, 1)

                @blk.scalar
                def _(e, ch=ch):
                    e.wait_ge(tv, bxtr[ch])
                    bxs_ev[ch] = bump(e.copy(
                        bxs[:, 4 * ch * N16:(4 * ch + 4) * N16], psbt[:]), ss, 1)

                @blk.vector
                def _(e, ch=ch):
                    e.wait_ge(tv, amm[ch])
                    for tau in range(4):
                        g = 4 * ch + tau
                        aev[g] = bump(e.tensor_copy(
                            atok[:, tau * 256:(tau + 1) * 256],
                            psa[:, tau * 256:(tau + 1) * 256]), vv, 1)

                    def AV(tau):
                        return atok[:, tau * 256:(tau + 1) * 256].rearrange(
                            "p (i j) -> p i j", i=16)

                    def T4(tau):
                        return tmp4k[:, tau * 256:(tau + 1) * 256]

                    def stage(mk):
                        for tau in range(4):
                            mk(tau, 4 * ch + tau)

                    B16 = [128, 16, 16]
                    stage(lambda tau, g: e.tensor_mul(
                        T4(tau).rearrange("p (i j) -> p i j", i=16), AV(tau),
                        u0s[:, g * N16:(g + 1) * N16].unsqueeze(2).broadcast_to(B16)))
                    stage(lambda tau, g: e.tensor_reduce(
                        piv[:, tau * 64:tau * 64 + 16],
                        T4(tau).rearrange("p (i j) -> p j i", i=16),
                        axis=AX.X, op=OP.add))
                    stage(lambda tau, g: e.tensor_mul(
                        T4(tau).rearrange("p (i j) -> p i j", i=16), AV(tau),
                        piv[:, tau * 64:tau * 64 + 16].unsqueeze(1).broadcast_to(B16)))
                    stage(lambda tau, g: e.tensor_reduce(
                        piv[:, tau * 64 + 16:tau * 64 + 32],
                        T4(tau).rearrange("p (i j) -> p i j", i=16),
                        axis=AX.X, op=OP.add))
                    stage(lambda tau, g: e.tensor_mul(
                        T4(tau).rearrange("p (i j) -> p i j", i=16), AV(tau),
                        piv[:, tau * 64 + 16:tau * 64 + 32].unsqueeze(2).broadcast_to(B16)))
                    stage(lambda tau, g: e.tensor_reduce(
                        piv[:, tau * 64 + 32:tau * 64 + 48],
                        T4(tau).rearrange("p (i j) -> p j i", i=16),
                        axis=AX.X, op=OP.add))
                    stage(lambda tau, g: e.tensor_mul(
                        T4(tau).rearrange("p (i j) -> p i j", i=16), AV(tau),
                        piv[:, tau * 64 + 32:tau * 64 + 48].unsqueeze(1).broadcast_to(B16)))
                    stage(lambda tau, g: e.tensor_reduce(
                        piv[:, tau * 64 + 48:tau * 64 + 64],
                        T4(tau).rearrange("p (i j) -> p i j", i=16),
                        axis=AX.X, op=OP.add))
                    stage(lambda tau, g: e.tensor_mul(
                        pisq[:, tau * 16:(tau + 1) * 16],
                        piv[:, tau * 64 + 32:tau * 64 + 48],
                        piv[:, tau * 64 + 32:tau * 64 + 48]))
                    stage(lambda tau, g: e.reduce_sum(
                        st_s[:, g:g + 1], pisq[:, tau * 16:(tau + 1) * 16],
                        axis=AX.X))
                    stage(lambda tau, g: e.tensor_mul(
                        pisq[:, tau * 16:(tau + 1) * 16],
                        piv[:, tau * 64 + 48:tau * 64 + 64],
                        piv[:, tau * 64 + 48:tau * 64 + 64]))
                    stage(lambda tau, g: e.reduce_sum(
                        st_t[:, g:g + 1], pisq[:, tau * 16:(tau + 1) * 16],
                        axis=AX.X))
                    stage(lambda tau, g: e.tensor_scalar_max(
                        st_t[:, g:g + 1], st_t[:, g:g + 1], 1e-30))
                    stage(lambda tau, g: e.reciprocal(
                        st_t[:, g:g + 1], st_t[:, g:g + 1]))
                    stage(lambda tau, g: e.scalar_tensor_tensor(
                        st_s[:, g:g + 1], st_s[:, g:g + 1], 0.09,
                        st_t[:, g:g + 1], op0=OP.mult, op1=OP.mult))
                    pv_ = bump(e.drain(), vv, 1)
                    for tau in range(4):
                        piq[4 * ch + tau] = pv_

                @blk.scalar
                def _(e, ch=ch):
                    e.wait_ge(vv, piq[4 * ch])
                    for tau in range(4):
                        g = 4 * ch + tau
                        e.activation(st_s[:, g:g + 1], st_s[:, g:g + 1], AF.Ln)
                    e.drain()
                    pd_ = 0
                    for tau in range(4):
                        g = 4 * ch + tau
                        pd_ = bump(e.activation(
                            st_s[:, g:g + 1], st_s[:, g:g + 1], AF.Exp,
                            scale=cvals[:, 1:2]), ss, 1)
                    for tau in range(4):
                        pisd[4 * ch + tau] = pd_

                for tau in range(4):
                    g = 4 * ch + tau

                    @blk.vector
                    def _(e, g=g, tau=tau):
                        A = atok[:, tau * 256:(tau + 1) * 256]
                        e.wait_ge(ss, pisd[g])
                        c = chain(e, vv)
                        c(lambda: e.tensor_scalar_min(
                            st_s[:, g:g + 1], st_s[:, g:g + 1], 1.0))
                        if dbg and g == 0:
                            e.tensor_copy(piv[:, 0:1], st_s[:, 0:1])
                        c(lambda: e.tensor_tensor(
                            A, A, st_s[:, g:g + 1].broadcast_to([128, 256]),
                            op=OP.mult))
                        if dbg and g == 0:
                            bump(e.tensor_copy(piv[:, 1:2], st_s[:, 0:1]), vv, 1)
                            dbg_smin = val(vv)

                            @blk.gpsimd
                            def _(e2):
                                e2.wait_ge(vv, dbg_smin)
                                with nc.allow_non_contiguous_dma(reason="dbg"):
                                    bump(e2.dma_start(out=dbg_pi[:, 66:67],
                                                      in_=piv[:, 0:1]), gg, 16)
                        t4 = tmp4k[:].rearrange("p (i j c) -> p i j c",
                                                i=16, j=16)
                        t4r = tmp4k[:].rearrange("p (i j c) -> p i c j",
                                                 i=16, j=16)

                        def mm16(dst, lf, rf):
                            c(lambda: e.tensor_tensor(
                                t4,
                                lf.rearrange("p (i j) -> p i j",
                                             i=16).unsqueeze(3).broadcast_to(
                                    [128, 16, 16, 16]),
                                rf.rearrange("p (j c) -> p j c",
                                             j=16).unsqueeze(1).broadcast_to(
                                    [128, 16, 16, 16]),
                                op=OP.mult))
                            return c(lambda: e.tensor_reduce(
                                dst.rearrange("p (i c) -> p i c", i=16),
                                t4r, axis=AX.X, op=OP.add))

                        mm16(a2b[:], A, A)
                        mm16(a3b[:], A, a2b[:])
                        mm16(a4b[:], a2b[:], a2b[:])
                        c(lambda: e.scalar_tensor_tensor(
                            wcmb[:], A, -2.0, itls[:],
                            op0=OP.mult, op1=OP.add))
                        c(lambda: e.scalar_tensor_tensor(
                            wcm2[:], a2b[:], 2.0, wcmb[:],
                            op0=OP.mult, op1=OP.add))
                        c(lambda: e.scalar_tensor_tensor(
                            wcmb[:], a3b[:], -2.0, wcm2[:],
                            op0=OP.mult, op1=OP.add))
                        wdn[g] = c(lambda: e.scalar_tensor_tensor(
                            wtok[:, g * 256:(g + 1) * 256], a4b[:], 1.0, wcmb[:],
                            op0=OP.mult, op1=OP.add), snap=True)

                if dbg and ch == 0:
                    @blk.gpsimd
                    def _(e):
                        e.wait_ge(vv, hev[3])
                        bump(e.dma_start(out=dbg_hdm, in_=hdm[:]), gg, 16)
                        e.wait_ge(vv, aev[3])
                        bump(e.dma_start(out=dbg_a, in_=atok[:]), gg, 16)
                        e.wait_ge(vv, vev[0])
                        bump(e.dma_start(out=dbg_v, in_=vsb[:]), gg, 16)
                        e.wait_ge(vv, wdn[3])
                        bump(e.dma_start(out=dbg_st[:, 0:NT], in_=st_m[:]), gg, 16)
                        bump(e.dma_start(out=dbg_st[:, NT:2 * NT], in_=st_r[:]), gg, 16)
                        bump(e.dma_start(out=dbg_st[:, 2 * NT:3 * NT], in_=st_mr[:]), gg, 16)
                        bump(e.dma_start(out=dbg_st[:, 3 * NT:4 * NT], in_=st_s[:]), gg, 16)

        barrier()

        eC = contextlib.ExitStack()
        wfcsb = eC.enter_context(
            nc.sbuf_tensor("wfcsb", [128, DS * 4 * D], BF16))

        @blk.gpsimd
        def _(e):
            for s in range(DS):
                wfc_done[0] = bump(
                    e.dma_start(out=wfcsb[:, s * 4 * D:(s + 1) * 4 * D],
                                in_=wfcT_d[s * 128:(s + 1) * 128, :]), gq, 16)

        e2 = contextlib.ExitStack()
        with e2:
            def S2(name, shape, dtp):
                return e2.enter_context(nc.sbuf_tensor(name, shape, dtp))

            hs = S2("hs", [128, N16], F32)
            hred = S2("hred", [128, N16], F32)
            tmp256 = S2("tmp256", [128, 256], F32)
            tmp4k2 = S2("tmp4k2", [128, 4096], F32)
            pmJ = S2("pmJ", [128, 256], F32)
            pmm = S2("pmm", [N16, 128 * N16], F32)
            hlocT = S2("hlocT", [N16, 128], F32)
            hbS = S2("hbS", [N16, 128], F32)
            hbSh = S2("hbSh", [N16, 128], F32)
            hinit = S2("hinit", [128, N16], F32)
            ps_s1 = e2.enter_context(nc.psum_tensor("ps_s1", [N16, 128], F32))
            ps_s2 = e2.enter_context(nc.psum_tensor("ps_s2", [128, N16], F32))
            psq2 = e2.enter_context(nc.psum_tensor("psq2", [N16, 2], F32))
            @blk.vector
            def _(e):
                c = chain(e, vv)
                c(lambda: e.tensor_copy(hs[:], bxs[:, 0:N16]))
                c(lambda: e.tensor_copy(
                    pmJ[:].rearrange("p (j i) -> p j i", j=16),
                    wtok[:, 0:256].rearrange("p (i j) -> p j i", i=16)))
                for s in range(1, L):
                    Wv = wtok[:, s * 256:(s + 1) * 256].rearrange(
                        "p (i j) -> p i j", i=16)
                    c(lambda Wv=Wv: e.tensor_mul(
                        tmp256[:].rearrange("p (i j) -> p i j", i=16), Wv,
                        hs[:].unsqueeze(1).broadcast_to([128, 16, 16])))
                    c(lambda: e.tensor_reduce(
                        hred[:], tmp256[:].rearrange("p (i j) -> p i j", i=16),
                        axis=AX.X, op=OP.add))
                    c(lambda s=s: e.tensor_add(
                        hs[:], hred[:], bxs[:, s * N16:(s + 1) * N16]))
                    c(lambda Wv=Wv: e.tensor_tensor(
                        tmp4k2[:].rearrange("p (i j c) -> p i j c", i=16, j=16),
                        Wv.unsqueeze(3).broadcast_to([128, 16, 16, 16]),
                        pmJ[:].rearrange("p (c j) -> p j c",
                                         c=16).unsqueeze(1).broadcast_to(
                            [128, 16, 16, 16]),
                        op=OP.mult))
                    c(lambda: e.tensor_reduce(
                        pmJ[:].rearrange("p (c i) -> p i c", c=16),
                        tmp4k2[:].rearrange("p (i j c) -> p i c j", i=16, j=16),
                        axis=AX.X, op=OP.add))
                bump(e.drain(), vv, 1)
            p1v = val(vv)

            @blk.gpsimd
            def _(e):
                e.wait_ge(vv, p1v)
                for j in range(N16):
                    bump(e.dma_start(out=pmm[j:j + 1, :],
                                     in_=pmJ[:, 16 * j:16 * (j + 1)]), gq, 16)
            pmm_done = val(gq)

            @blk.tensor
            def _(e):
                e.wait_ge(vv, p1v)
                bump(e.matmul(ps_s1[:], hs[:], if32[:], start=True, stop=True),
                     tv, 1)
            hlt_mm = val(tv)

            @blk.vector
            def _(e):
                e.wait_ge(tv, hlt_mm)
                c = chain(e, vv)
                c(lambda: e.tensor_copy(hlocT[:], ps_s1[:]))
                c(lambda: e.tensor_copy(
                    hbS[:].rearrange("p (b c) -> p b c", b=2)[:, :, 0:1],
                    hlocT[:].rearrange("p (b c) -> p b c", b=2)[:, :, 0:1]))
            hbup = [0] * CCH
            hbup[0] = val(vv)

            for c in range(1, CCH):
                @blk.tensor
                def _(e, c=c):
                    e.wait_ge(vv, hbup[c - 1])
                    if c == 1:
                        e.wait_ge(gq, pmm_done)
                    for b in range(NB):
                        inst = b * CCH + c
                        bump(e.matmul(psq2[:, b:b + 1],
                                      pmm[:, inst * N16:(inst + 1) * N16],
                                      hbS[:, b * CCH + c - 1:b * CCH + c],
                                      start=True, stop=True), tv, 1)
                mmv = val(tv)

                @blk.vector
                def _(e, c=c, mmv=mmv):
                    e.wait_ge(tv, mmv)
                    hbup[c] = bump(e.tensor_add(
                        hbS[:].rearrange("p (b c) -> p b c", b=2)[:, :, c:c + 1],
                        psq2[:].unsqueeze(2),
                        hlocT[:].rearrange("p (b c) -> p b c",
                                           b=2)[:, :, c:c + 1]), vv, 1)

            @blk.vector
            def _(e):
                e.wait_ge(vv, hbup[CCH - 1])
                c = chain(e, vv)
                c(lambda: e.memset(hbSh[:], 0.0))
                c(lambda: e.tensor_copy(hbSh[:, 1:CCH], hbS[:, 0:CCH - 1]))
                c(lambda: e.tensor_copy(hbSh[:, CCH + 1:2 * CCH],
                                        hbS[:, CCH:2 * CCH - 1]), snap=True)
            hsh_v = val(vv)

            @blk.tensor
            def _(e):
                e.wait_ge(vv, hsh_v)
                bump(e.matmul(ps_s2[:], hbSh[:], i16s[:], start=True, stop=True),
                     tv, 1)
            hin_mm = val(tv)

            hin_v_box = [0]

            @blk.vector
            def _(e):
                e.wait_ge(tv, hin_mm)
                c = chain(e, vv)
                hin_v_box[0] = c(lambda: e.tensor_copy(hinit[:], ps_s2[:]),
                                 snap=True)
                for s in range(L):
                    Wv = wtok[:, s * 256:(s + 1) * 256].rearrange(
                        "p (i j) -> p i j", i=16)
                    prev = hinit[:] if s == 0 else hall[:, (s - 1) * N16:s * N16]
                    c(lambda Wv=Wv, prev=prev: e.tensor_mul(
                        tmp256[:].rearrange("p (i j) -> p i j", i=16), Wv,
                        prev.unsqueeze(1).broadcast_to([128, 16, 16])))
                    c(lambda: e.tensor_reduce(
                        hred[:], tmp256[:].rearrange("p (i j) -> p i j", i=16),
                        axis=AX.X, op=OP.add))
                    c(lambda s=s: e.tensor_add(
                        hall[:, s * N16:(s + 1) * N16], hred[:],
                        bxs[:, s * N16:(s + 1) * N16]))
                bump(e.drain(), vv, 1)
            p3v = val(vv)

            htl = [0] * NT
            hte = [0] * NT
            for g in range(NT):
                @blk.tensor
                def _(e, g=g):
                    if g == 0:
                        e.wait_ge(vv, p3v)
                    else:
                        e.wait_ge(vv, hte[g - 1])
                    htl[g] = bump(e.matmul(
                        ps_s1[:], hall[:, g * N16:(g + 1) * N16], if32[:],
                        start=True, stop=True), tv, 1)

                @blk.vector
                def _(e, g=g):
                    e.wait_ge(tv, htl[g])
                    hte[g] = bump(e.tensor_copy(
                        hallT[:, g * 128:(g + 1) * 128], ps_s1[:]), vv, 1)

            if dbg:
                @blk.gpsimd
                def _(e):
                    e.wait_ge(vv, p1v)
                    bump(e.dma_start(out=dbg_sc2[:, 0:256], in_=pmJ[:]), gg, 16)
                    bump(e.dma_start(out=dbg_sc2[:, 256:272], in_=hs[:]), gg, 16)
                    e.wait_ge(vv, hin_v_box[0])
                    bump(e.dma_start(out=dbg_sc2[:, 272:288], in_=hinit[:]), gg, 16)
                    bump(e.dma_start(out=dbg_hb[:, 0:128], in_=hlocT[:]), gg, 16)
                    bump(e.dma_start(out=dbg_hb[:, 128:256], in_=hbS[:]), gg, 16)
                    bump(e.dma_start(out=dbg_hb[:, 256:384], in_=hbSh[:]), gg, 16)
                    e.wait_ge(vv, hte[NT - 1])
                    bump(e.dma_start(out=dbg_w, in_=wtok[:]), gg, 16)
                    bump(e.dma_start(out=dbg_hall, in_=hall[:]), gg, 16)
                    bump(e.dma_start(out=dbg_bxs, in_=bxs[:]), gg, 16)
                    bump(e.dma_start(out=dbg_sc, in_=st_s[:]), gg, 16)

        barrier()
        eB.close()

        e3 = contextlib.ExitStack()
        with e3:
            def S3(name, shape, dtp):
                return e3.enter_context(nc.sbuf_tensor(name, shape, dtp))

            x2b = S3("x2b", [128, TCH2 * D], F32)
            xn2b = S3("xn2b", [128, D], BF16)
            t2a = S3("t2a", [128, D], F32)
            xmid = S3("xmid", [128, 3 * TCH2 * D], F32)
            ycp = S3("ycp", [128, DS * NTK], F32)
            h2t = S3("h2t", [128, TCH2 * D], BF16)
            x2d = S3("x2d", [128, 2 * DS * NTK], BF16)
            actb = S3("actb", [128, 8 * NTK], BF16)
            relb = S3("relb", [128, 2 * NTK], BF16)
            memb = S3("memb", [128, DS * NTK], F32)
            memc = S3("memc", [128, DS * NTK], F32)
            sp1 = S3("sp1", [128, DS * NTK], BF16)
            sp2 = S3("sp2", [128, DS * NTK], BF16)
            spka = S3("spka", [128, 2 * DS * NTK], BF16)
            spkt = S3("spkt", [128, TCH2 * D], BF16)
            wprb = S3("wprb", [128, 4 * D], BF16)
            ps2t = e3.enter_context(nc.psum_tensor("ps2t", [128, D], F32))
            psy1 = ps2t
            pstr2 = ps2t
            psfc = e3.enter_context(nc.psum_tensor("psfc", [128, 4 * 256], F32))
            pspj = e3.enter_context(nc.psum_tensor("pspj", [128, DS * NTK], F32))
            x2ld = [0] * NCH2
            y1mm = [[0] * TCH2 for _ in range(NCH2)]
            xmd = [[0] * TCH2 for _ in range(NCH2)]
            h2dn = [[0] * TCH2 for _ in range(NCH2)]
            h2tr = [[0] * TCH2 for _ in range(NCH2)]
            x2dev = [[0] * TCH2 for _ in range(NCH2)]
            fcmm = [[0] * FS for _ in range(NCH2)]
            reld = [[0] * FS for _ in range(NCH2)]
            wld = [[0] * FS for _ in range(NCH2)]
            pjmm = [[0] * FS for _ in range(NCH2)]
            spdone = [0] * NCH2
            sptr = [[0] * TCH2 for _ in range(NCH2)]
            spev = [[0] * TCH2 for _ in range(NCH2)]
            findone = [[0] * TCH2 for _ in range(NCH2)]
            stdone = [[0] * TCH2 for _ in range(NCH2)]
            copyd = [[0] * FS for _ in range(NCH2)]
            relsq = [[0] * FS for _ in range(NCH2)]
            ycpd = [0] * NCH2

            import os as _os
            NCH2_RUN = int(_os.environ.get('NCH2_RUN', NCH2))
            LASTP = NCH2_RUN - 1

            def emit_preA1(Y):
                @blk.sync
                def _(e, Y=Y):
                    if Y >= 1:
                        e.wait_ge(vv, xmd[Y - 1][TCH2 - 1])
                    for tau in range(TCH2):
                        x2ld[Y] = bump(e.dma_start(
                            out=x2b[:, tau * D:(tau + 1) * D],
                            in_=xr[TCH2 * Y + tau]), sq, 16)

                for tau in range(1):
                    g = TCH2 * Y + tau

                    @blk.tensor
                    def _(e, Y=Y, tau=tau, g=g):
                        if Y >= 1:
                            e.wait_ge(vv, x2dev[Y - 1][TCH2 - 1])
                        if Y >= 3:
                            e.wait_ge(vv, spev[Y - 3][TCH2 - 1])
                        for n in range(2):
                            ps = psy1[:, n * 512:(n + 1) * 512]
                            bump(e.matmul(ps, bselb[:],
                                          gdsa[0:2, n * 512:(n + 1) * 512],
                                          start=True, stop=False), tv, 1)
                            bump(e.matmul(psy1[0:64, n * 512:(n + 1) * 512],
                                          hallT[:, g * 128:g * 128 + 64],
                                          ctg[:, n * 512:(n + 1) * 512],
                                          start=False, stop=False,
                                          skip_group_check=True), tv, 1)
                            y1mm[Y][tau] = bump(e.matmul(
                                psy1[64:128, n * 512:(n + 1) * 512],
                                hallT[:, g * 128 + 64:(g + 1) * 128],
                                ctg[:, D + n * 512:D + (n + 1) * 512],
                                start=False, stop=True,
                                skip_group_check=True), tv, 1)

            def emit_preA2(Y):
                cslx = Y % 3
                xmsums = [0] * TCH2

                for tau in range(TCH2):
                    g = TCH2 * Y + tau
                    XSL = slice((cslx * TCH2 + tau) * D,
                                (cslx * TCH2 + tau + 1) * D)

                    if tau > 0:
                        @blk.tensor
                        def _(e, Y=Y, tau=tau, g=g):
                            e.wait_ge(vv, xmd[Y][tau - 1])
                            for n in range(2):
                                ps = psy1[:, n * 512:(n + 1) * 512]
                                bump(e.matmul(ps, bselb[:],
                                              gdsa[0:2, n * 512:(n + 1) * 512],
                                              start=True, stop=False), tv, 1)
                                bump(e.matmul(
                                    psy1[0:64, n * 512:(n + 1) * 512],
                                    hallT[:, g * 128:g * 128 + 64],
                                    ctg[:, n * 512:(n + 1) * 512],
                                    start=False, stop=False,
                                    skip_group_check=True), tv, 1)
                                y1mm[Y][tau] = bump(e.matmul(
                                    psy1[64:128, n * 512:(n + 1) * 512],
                                    hallT[:, g * 128 + 64:(g + 1) * 128],
                                    ctg[:, D + n * 512:D + (n + 1) * 512],
                                    start=False, stop=True,
                                    skip_group_check=True), tv, 1)

                    @blk.vector
                    def _(e, Y=Y, tau=tau, g=g):
                        e.wait_ge(sq, x2ld[Y] - 16 * (TCH2 - 1 - tau))
                        c = chain(e, vv)
                        c(lambda: e.tensor_tensor(
                            xn2b[:], x2b[:, tau * D:(tau + 1) * D],
                            st_r[:, g:g + 1].broadcast_to([128, D]), op=OP.mult))
                        c(lambda: e.tensor_tensor(
                            xn2b[:], xn2b[:],
                            st_mr[:, g:g + 1].broadcast_to([128, D]),
                            op=OP.add))

                    @blk.vector
                    def _(e, Y=Y, tau=tau, g=g, XSL=XSL):
                        xt = x2b[:, tau * D:(tau + 1) * D]
                        xm = xmid[:, XSL]
                        if Y >= 3 and tau == 0:
                            e.wait_ge(so, stdone[Y - 3][TCH2 - 1])
                        e.drain()
                        c = chain(e, vv)
                        c(lambda: e.tensor_mul(t2a[:], xn2b[:], gdca_t[:]))
                        c(lambda: e.tensor_add(t2a[:], t2a[:], xt))
                        e.wait_ge(tv, y1mm[Y][tau])
                        xmd[Y][tau] = c(lambda: e.tensor_add(xm, t2a[:],
                                                             psy1[:]),
                                        snap=True)
                        c(lambda: e.reduce_sum(st2_m[:, g:g + 1], xm,
                                               axis=AX.X), snap=True)
                    xmsums[tau] = val(vv)

                for tau in range(TCH2):
                    g = TCH2 * Y + tau
                    XSL = slice((cslx * TCH2 + tau) * D,
                                (cslx * TCH2 + tau + 1) * D)

                    @blk.vector
                    def _(e, g=g, XSL=XSL):
                        c = chain(e, vv)
                        c(lambda: e.tensor_mul(t2a[:], xmid[:, XSL],
                                               xmid[:, XSL]))
                        c(lambda: e.reduce_sum(st2_r[:, g:g + 1], t2a[:],
                                               axis=AX.X))
                        c(lambda: e.tensor_scalar_mul(
                            st2_m[:, g:g + 1], st2_m[:, g:g + 1], 1.0 / D))
                        c(lambda: e.tensor_scalar_mul(
                            st2_r[:, g:g + 1], st2_r[:, g:g + 1], 1.0 / D))
                        c(lambda: e.tensor_mul(st2_mr[:, g:g + 1],
                                               st2_m[:, g:g + 1],
                                               st2_m[:, g:g + 1]))
                        c(lambda: e.tensor_sub(st2_r[:, g:g + 1],
                                               st2_r[:, g:g + 1],
                                               st2_mr[:, g:g + 1]))
                        c(lambda: e.tensor_scalar_add(
                            st2_r[:, g:g + 1], st2_r[:, g:g + 1], 1e-5),
                          snap=True)
                    var2_d = val(vv)

                    @blk.scalar
                    def _(e, g=g, var2_d=var2_d):
                        e.wait_ge(vv, var2_d)
                        e.activation(st2_r[:, g:g + 1], st2_r[:, g:g + 1],
                                     AF.Ln)
                        e.drain()
                        bump(e.activation(st2_r[:, g:g + 1], st2_r[:, g:g + 1],
                                          AF.Exp, scale=cvals[:, 0:1]), ss, 1)
                    sd2_d = val(ss)

                    @blk.vector
                    def _(e, g=g, sd2_d=sd2_d):
                        e.wait_ge(ss, sd2_d)
                        bump(e.scalar_tensor_tensor(
                            st2_mr[:, g:g + 1], st2_m[:, g:g + 1], -1.0,
                            st2_r[:, g:g + 1], op0=OP.mult, op1=OP.mult), vv, 1)
                    r2_d = val(vv)

                    @blk.vector
                    def _(e, Y=Y, g=g, tau=tau, r2_d=r2_d, XSL=XSL):
                        e.wait_ge(vv, r2_d)
                        c = chain(e, vv)
                        c(lambda: e.tensor_tensor(
                            xn2b[:], xmid[:, XSL],
                            st2_r[:, g:g + 1].broadcast_to([128, D]),
                            op=OP.mult))
                        c(lambda: e.tensor_tensor(
                            xn2b[:], xn2b[:],
                            st2_mr[:, g:g + 1].broadcast_to([128, D]),
                            op=OP.add))
                        ht = h2t[:, tau * D:(tau + 1) * D]
                        c(lambda: e.tensor_mul(ht, xn2b[:], cb1_bc[:]))
                        h2dn[Y][tau] = c(lambda: e.tensor_add(ht, ht,
                                                              sb_bc[:]),
                                         snap=True)

            def emit_preB(Y):
                csld = Y % 2
                for tau in range(TCH2):
                    @blk.tensor
                    def _(e, Y=Y, tau=tau):
                        e.wait_ge(vv, h2dn[Y][tau])
                        e.wait_ge(vv, xmd[Y][TCH2 - 1])
                        if tau > 0:
                            e.wait_ge(vv, x2dev[Y][tau - 1])
                        for sdx in range(DS):
                            h2tr[Y][tau] = bump(e.matmul(
                                pstr2[:, sdx * 128:(sdx + 1) * 128],
                                h2t[:, tau * D + sdx * 128:
                                    tau * D + (sdx + 1) * 128],
                                i128[:], start=True, stop=True), tv, 1)

                    @blk.vector
                    def _(e, Y=Y, tau=tau, csld=csld):
                        e.wait_ge(tv, h2tr[Y][tau])
                        if Y >= 2 and tau == 0:
                            e.wait_ge(tv, fcmm[Y - 2][FS - 1])
                        x2dev[Y][tau] = bump(e.tensor_copy(
                            x2d[:, csld * DS * NTK:
                                (csld + 1) * DS * NTK].rearrange(
                                "p (s t) -> p s t",
                                s=DS)[:, :, tau * 128:(tau + 1) * 128],
                            pstr2[:].rearrange("p (s t) -> p s t", s=DS)),
                            vv, 1)

            def emit_ffn(ch):
                csld = ch % 2
                for f in range(FS):
                    fsl = f % 2

                    @blk.sync
                    def _(e, ch=ch, f=f, fsl=fsl):
                        if ch == 0 and f == 0:
                            e.wait_ge(gq, wpr_done[0])
                        if f >= 2:
                            e.wait_ge(tv, pjmm[ch][f - 2])
                        elif ch > 0:
                            e.wait_ge(tv, pjmm[ch - 1][FS - 2 + f])
                        wld[ch][f] = bump(e.dma_start(
                            out=wprb[:, fsl * D:(fsl + 1) * D],
                            in_=wprB_d[f * 128:(f + 1) * 128, :]), sq, 16)

                    @blk.tensor
                    def _(e, ch=ch, f=f, csld=csld):
                        if f == 0:
                            e.wait_ge(vv, x2dev[ch][TCH2 - 1])
                            e.wait_ge(gq, wfc_done[0])
                        if f >= 2:
                            e.wait_ge(ss, copyd[ch][f - 2])
                        elif ch > 0:
                            e.wait_ge(ss, copyd[ch - 1][FS - 2 + f])
                        ps = psfc[:, (f % 2) * 256:(f % 2) * 256 + 256]
                        for k in range(DS):
                            fcmm[ch][f] = bump(e.matmul(
                                ps, wfcsb[:, k * 4 * D + f * 128:
                                          k * 4 * D + (f + 1) * 128],
                                x2d[:, csld * DS * NTK + k * NTK:
                                    csld * DS * NTK + (k + 1) * NTK],
                                start=(k == 0), stop=(k == DS - 1)), tv, 1)

                    @blk.scalar
                    def _(e, ch=ch, f=f):
                        e.wait_ge(tv, fcmm[ch][f])
                        if f >= 2:
                            e.wait_ge(gs, relsq[ch][f - 2])
                        elif ch > 0:
                            e.wait_ge(gs, relsq[ch - 1][FS - 2 + f])
                        ps = psfc[:, (f % 2) * 256:(f % 2) * 256 + 256]
                        rb = relb[:, (f % 2) * NTK:(f % 2 + 1) * NTK]
                        copyd[ch][f] = bump(e.activation(rb, ps, AF.Copy),
                                            ss, 1)

                    @blk.gpsimd
                    def _(e, ch=ch, f=f):
                        e.wait_ge(ss, copyd[ch][f])
                        if f >= 8:
                            e.wait_ge(tv, pjmm[ch][f - 8])
                        elif ch > 0:
                            e.wait_ge(tv, pjmm[ch - 1][FS - 8 + f])
                        rb = relb[:, (f % 2) * NTK:(f % 2 + 1) * NTK]
                        e.tensor_scalar_max(rb, rb, 0.0)
                        e.drain()
                        relsq[ch][f] = bump(e.tensor_mul(
                            actb[:, (f % 8) * NTK:(f % 8 + 1) * NTK],
                            rb, rb), gs, 1)

                    @blk.tensor
                    def _(e, ch=ch, f=f, fsl=fsl):
                        e.wait_ge(gs, relsq[ch][f])
                        e.wait_ge(sq, wld[ch][f])
                        if f == 0 and ch > 0:
                            e.wait_ge(vv, ycpd[ch - 1])
                        for dd in range(DS):
                            pjmm[ch][f] = bump(e.matmul(
                                pspj[:, dd * NTK:(dd + 1) * NTK],
                                wprb[:, fsl * D + dd * 128:
                                     fsl * D + (dd + 1) * 128],
                                actb[:, (f % 8) * NTK:(f % 8 + 1) * NTK],
                                start=(f == 0), stop=(f == FS - 1)), tv, 1)

            def emit_ycp(ch):
                @blk.vector
                def _(e, ch=ch):
                    e.wait_ge(tv, pjmm[ch][FS - 1])
                    for dd in range(DS):
                        e.tensor_copy(ycp[:, dd * NTK:(dd + 1) * NTK],
                                      pspj[:, dd * NTK:(dd + 1) * NTK])
                    ycpd[ch] = bump(e.drain(), vv, 1)

            def emit_spike(ch):
                @blk.vector
                def _(e, ch=ch):
                    if ch >= 2:
                        e.wait_ge(tv, sptr[ch - 2][TCH2 - 1])
                    sb = (ch % 2) * DS * NTK

                    def SL(t, dd):
                        return t[:, dd * NTK:(dd + 1) * NTK]

                    def SK(dd):
                        return spka[:, sb + dd * NTK:sb + (dd + 1) * NTK]

                    def stage(mk):
                        for dd in range(DS):
                            mk(dd)
                        e.drain()

                    stage(lambda dd: e.tensor_single_scalar(
                        SL(sp1, dd), SL(ycp, dd), 1.0, op=OP.is_gt))
                    stage(lambda dd: e.scalar_tensor_tensor(
                        SL(memb, dd), SL(ycp, dd), 1.9, SL(sp1, dd),
                        op0=OP.mult, op1=OP.subtract))
                    stage(lambda dd: e.tensor_single_scalar(
                        SL(sp2, dd), SL(memb, dd), 1.0, op=OP.is_gt))
                    stage(lambda dd: e.tensor_add(
                        SK(dd), SL(sp1, dd), SL(sp2, dd)))
                    seq = [(memb, memc, sp2, sp1), (memc, memb, sp1, sp2),
                           (memb, memc, sp2, sp1)]
                    for (cur, nxt, spc, spn) in seq:
                        stage(lambda dd, cur=cur, nxt=nxt: e.scalar_tensor_tensor(
                            SL(nxt, dd), SL(cur, dd), 0.9, SL(ycp, dd),
                            op0=OP.mult, op1=OP.add))
                        stage(lambda dd, nxt=nxt, spc=spc: e.tensor_sub(
                            SL(nxt, dd), SL(nxt, dd), SL(spc, dd)))
                        stage(lambda dd, nxt=nxt, spn=spn: e.tensor_single_scalar(
                            SL(spn, dd), SL(nxt, dd), 1.0, op=OP.is_gt))
                        stage(lambda dd, spn=spn: e.tensor_add(
                            SK(dd), SK(dd), SL(spn, dd)))
                    for b in range(NB):
                        stage(lambda dd, b=b: e.tensor_scalar_mul(
                            SK(dd).rearrange(
                                "p (t bb cc) -> p t bb cc",
                                t=TCH2, bb=2)[:, :, b, :],
                            SK(dd).rearrange(
                                "p (t bb cc) -> p t bb cc",
                                t=TCH2, bb=2)[:, :, b, :],
                            gbsT[:, dd * 2 + b:dd * 2 + b + 1]))
                    bump(e.drain(), vv, 1)
                spdone[ch] = val(vv)

            def emit_tail(X):
                cslx = X % 3
                sb = (X % 2) * DS * NTK
                for tau in range(TCH2):
                    @blk.tensor
                    def _(e, X=X, tau=tau, sb=sb):
                        e.wait_ge(vv, spdone[X])
                        if tau > 0:
                            e.wait_ge(vv, spev[X][tau - 1])
                        else:
                            if X + 2 <= LASTP:
                                e.wait_ge(vv, x2dev[X + 2][TCH2 - 1])
                            elif X >= 1:
                                e.wait_ge(vv, spev[X - 1][TCH2 - 1])
                            else:
                                e.wait_ge(vv, x2dev[LASTP][TCH2 - 1])
                        for dd in range(DS):
                            sptr[X][tau] = bump(e.matmul(
                                pstr2[:, dd * 128:(dd + 1) * 128],
                                spka[:, sb + dd * NTK + tau * 128:
                                     sb + dd * NTK + (tau + 1) * 128],
                                i128[:], start=True, stop=True), tv, 1)

                    @blk.vector
                    def _(e, X=X, tau=tau):
                        e.wait_ge(tv, sptr[X][tau])
                        spev[X][tau] = bump(e.tensor_copy(
                            spkt[:, tau * D:(tau + 1) * D], pstr2[:]), vv, 1)

                for tau in range(TCH2):
                    g = TCH2 * X + tau
                    XSL = slice((cslx * TCH2 + tau) * D,
                                (cslx * TCH2 + tau + 1) * D)

                    @blk.vector
                    def _(e, X=X, tau=tau, XSL=XSL):
                        e.wait_ge(vv, spev[X][tau])
                        xm = xmid[:, XSL]
                        findone[X][tau] = bump(e.tensor_add(
                            xm, spkt[:, tau * D:(tau + 1) * D], xm), vv, 1)

                    @blk.scalar
                    def _(e, X=X, tau=tau, g=g, XSL=XSL):
                        e.wait_ge(vv, findone[X][tau])
                        stdone[X][tau] = bump(e.dma_start(
                            out=orr[g], in_=xmid[:, XSL]), so, 16)

            def emit_dbg(ch):
                if ch == 0:
                    @blk.gpsimd
                    def _(e):
                        e.wait_ge(vv, xmd[0][TCH2 - 1])
                        bump(e.dma_start(out=dbg_xmid,
                                         in_=xmid[:, 0:TCH2 * D]), gg, 16)

            if NCH2_RUN > 0:
                emit_preA1(0)
                emit_preA2(0)
                emit_preB(0)
            DBGSTOP = bool(_os.environ.get('DBGSTOP'))
            for ch in range(NCH2_RUN):
                if ch >= 2:
                    emit_tail(ch - 2)
                if ch + 1 <= LASTP:
                    emit_preA1(ch + 1)
                if not DBGSTOP:
                    emit_ffn(ch)
                if ch + 1 <= LASTP:
                    emit_preA2(ch + 1)
                    emit_preB(ch + 1)
                if not DBGSTOP:
                    emit_ycp(ch)
                    emit_spike(ch)
                if dbg:
                    emit_dbg(ch)
            if not DBGSTOP:
                if NCH2_RUN >= 2:
                    emit_tail(NCH2_RUN - 2)
                if NCH2_RUN >= 1:
                    emit_tail(NCH2_RUN - 1)

            @blk.sync
            def _(e):
                e.wait_ge(so, val(so))

        eC.close()

    return nc


def _make_consts():
    rows, cols = np.triu_indices(N16, 1)
    mc = np.zeros((P120, 256), np.float32)
    for k in range(P120):
        mc[k, 16 * rows[k] + cols[k]] = 1.0
        mc[k, 16 * cols[k] + rows[k]] = -1.0
    i128f = np.eye(128, dtype=np.float32)
    i16 = np.eye(16, dtype=np.float32)
    itile = np.tile(np.eye(16, dtype=np.float32).reshape(1, 256), (128, 1))
    knl = np.zeros((4, 128), np.float32)
    knl[0] = 1.0
    bsel = np.zeros((2, 128), np.float32)
    bsel[0, 0:64] = 1.0
    bsel[1, 64:128] = 1.0
    esel = np.zeros((2, 32), np.float32)
    esel[0, 0:16] = 1.0
    esel[1, 16:32] = 1.0
    return mc, i128f, i16, np.ascontiguousarray(itile), knl, bsel, esel


def _u0_global(np_dtype):
    import jax
    import jax.numpy as jnp
    cpu = jax.devices("cpu")[0]
    with jax.default_device(cpu):
        key = jax.random.key(1, impl="threefry2x32")
        if np_dtype == np.float64:
            from jax.experimental import enable_x64
            with enable_x64():
                key = jax.random.key(1, impl="threefry2x32")
                u = jax.random.normal(key, (16, T, N16, 1), dtype=jnp.float64)
                n = jnp.maximum(jnp.linalg.norm(u, axis=2, keepdims=True),
                                1e-12)
                return np.asarray(u / n)[..., 0]
        u = jax.random.normal(key, (16, T, N16, 1), dtype=jnp.float32)
        n = jnp.maximum(jnp.linalg.norm(u, axis=2, keepdims=True), 1e-12)
        return np.asarray(u / n)[..., 0]


def scan_layout(a):
    k = a.shape[-1]
    v = a.reshape(2, CCH, L, k)
    v = np.moveaxis(v, 2, 0).reshape(L, 128, k)
    v = np.moveaxis(v, 0, 1).reshape(128, L * k)
    return np.ascontiguousarray(v)


def kernel(L0_attn_u, L0_attn_W_proj_w, L0_attn_W_proj_b, L0_attn_B_w,
           L0_attn_C_w, L0_attn_D, L0_ffn_W_fc, L0_ffn_W_proj, L0_c, L0_wmod,
           _dbg=False):
    import ml_dtypes
    xf = np.asarray(L0_attn_u)
    mc, i128f, i16, itile, knl, bsel, esel = _make_consts()
    u0g = _u0_global(xf.dtype.type)

    nc = build_nc(dbg=_dbg)

    wpT = np.ascontiguousarray(np.asarray(L0_attn_W_proj_w).T)
    wpb = np.ascontiguousarray(np.asarray(L0_attn_W_proj_b).reshape(P120, 1))
    bwT = np.ascontiguousarray(np.asarray(L0_attn_B_w).T)
    cT = np.ascontiguousarray(np.asarray(L0_attn_C_w).T)
    dv = np.ascontiguousarray(np.asarray(L0_attn_D).reshape(1, D))
    wfcT = np.ascontiguousarray(np.asarray(L0_ffn_W_fc).T)
    wprT = np.ascontiguousarray(np.asarray(L0_ffn_W_proj).T)
    wmod = np.ascontiguousarray(np.asarray(L0_wmod))
    cmod_full = np.asarray(L0_c)

    in_maps = []
    for c in range(8):
        xs = np.ascontiguousarray(xf[2 * c:2 * c + 2])
        u0c = scan_layout(
            np.ascontiguousarray(u0g[2 * c:2 * c + 2]).astype(xs.dtype))
        in_maps.append({
            "x": xs,
            "wpT": wpT, "wpb": wpb, "bwT": bwT, "cT": cT, "dvec": dv,
            "wfcT": wfcT, "wprT": wprT,
            "cmod": np.ascontiguousarray(cmod_full[2 * c:2 * c + 2]),
            "wmod": wmod,
            "mc": mc, "i128b": i128f.astype(ml_dtypes.bfloat16),
            "i128f": i128f, "i16": i16, "itile": itile,
            "cvals": np.tile(np.array([[-0.5, 0.5]], np.float32), (128, 1)),
            "knl": knl, "bsel": bsel,
            "bselb": bsel.astype(ml_dtypes.bfloat16), "esel": esel,
            "u0": u0c,
        })

    res = run_bass_kernel_spmd(nc, in_maps, core_ids=list(range(8)))
    outs = [np.asarray(r["out"]) for r in res.results]
    if _dbg:
        kernel._dbg_results = res.results
    return np.concatenate(outs, axis=0)


# version 08
# speedup vs baseline: 1.5813x; 1.5813x over previous
import os
import numpy as np
import concourse.bass as bass
import concourse.mybir as mybir
from concourse.bass_utils import run_bass_kernel_spmd

F32 = mybir.dt.float32
BF16 = mybir.dt.bfloat16
AF = mybir.ActivationFunctionType
OP = mybir.AluOpType
AX = mybir.AxisListType

NB = 2
T = 2048
D = 1024
N16 = 16
P120 = 120
L = 32
CCH = 64
NT = 32
DS = 8
FS = 32
TCH2 = 2
NCH2 = NT // TCH2
NTK = TCH2 * 128


def build_nc(dbg=False):
    nc = bass.Bass("TRN2", target_bir_lowering=False)
    dt = nc.dram_tensor

    x_in = dt("x", [NB, T, D], F32, kind="ExternalInput").ap()
    wpT_d = dt("wpT", [D, P120], F32, kind="ExternalInput").ap()
    wpb_d = dt("wpb", [P120, 1], F32, kind="ExternalInput").ap()
    bwT_d = dt("bwT", [D, N16], F32, kind="ExternalInput").ap()
    cT_d = dt("cT", [N16, D], F32, kind="ExternalInput").ap()
    dv_d = dt("dvec", [1, D], F32, kind="ExternalInput").ap()
    wfcT_d = dt("wfcT", [D, 4 * D], F32, kind="ExternalInput").ap()
    wprT_d = dt("wprT", [4 * D, D], F32, kind="ExternalInput").ap()
    cmod_d = dt("cmod", [NB, D], F32, kind="ExternalInput").ap()
    wmod_d = dt("wmod", [D, 6 * D], F32, kind="ExternalInput").ap()
    mc_d = dt("mc", [P120, 256], F32, kind="ExternalInput").ap()
    i128b_d = dt("i128b", [128, 128], BF16, kind="ExternalInput").ap()
    i128f_d = dt("i128f", [128, 128], F32, kind="ExternalInput").ap()
    i16_d = dt("i16", [N16, N16], F32, kind="ExternalInput").ap()
    itile_d = dt("itile", [128, 256], F32, kind="ExternalInput").ap()
    knl_d = dt("knl", [4, 128], F32, kind="ExternalInput").ap()
    bsel_d = dt("bsel", [2, 128], F32, kind="ExternalInput").ap()
    bselb_d = dt("bselb", [2, 128], BF16, kind="ExternalInput").ap()
    esel_d = dt("esel", [2, 32], F32, kind="ExternalInput").ap()
    u0_d = dt("u0", [128, NT * N16], F32, kind="ExternalInput").ap()
    cvals_d = dt("cvals", [128, 2], F32, kind="ExternalInput").ap()

    out_d = dt("out", [NB, T, D], F32, kind="ExternalOutput").ap()
    wprB_d = dt("wprB", [4 * D, D], BF16, kind="Internal").ap()

    if dbg:
        dbg_w = dt("dbg_w", [128, NT * 256], F32, kind="ExternalOutput").ap()
        dbg_hall = dt("dbg_hall", [128, NT * N16], F32, kind="ExternalOutput").ap()
        dbg_bxs = dt("dbg_bxs", [128, NT * N16], F32, kind="ExternalOutput").ap()
        dbg_sc = dt("dbg_sc", [128, NT], F32, kind="ExternalOutput").ap()
        dbg_a = dt("dbg_a", [128, 4 * 256], F32, kind="ExternalOutput").ap()
        dbg_v = dt("dbg_v", [P120, 512], F32, kind="ExternalOutput").ap()
        dbg_mod = dt("dbg_mod", [NB, 6 * D], F32, kind="ExternalOutput").ap()
        dbg_xmid = dt("dbg_xmid", [128, TCH2 * D], F32, kind="ExternalOutput").ap()
        dbg_bc = dt("dbg_bc", [128, 2 * D], F32, kind="ExternalOutput").ap()
        dbg_st = dt("dbg_st", [128, 4 * NT], F32, kind="ExternalOutput").ap()
        dbg_hdm = dt("dbg_hdm", [128, DS * 512], BF16, kind="ExternalOutput").ap()
        dbg_xn = dt("dbg_xn", [128, D], BF16, kind="ExternalOutput").ap()
        dbg_htk = dt("dbg_htk", [128, D], BF16, kind="ExternalOutput").ap()
        dbg_st0 = dt("dbg_st0", [128, 8], F32, kind="ExternalOutput").ap()
        dbg_pi = dt("dbg_pi", [128, 72], F32, kind="ExternalOutput").ap()
        dbg_sc2 = dt("dbg_sc2", [128, 256 + 16 + 16], F32, kind="ExternalOutput").ap()
        dbg_hb = dt("dbg_hb", [N16, 3 * 128], F32, kind="ExternalOutput").ap()

    xr = x_in.rearrange("b (c s) d -> s (b c) d", s=L)
    orr = out_d.rearrange("b (c s) d -> s (b c) d", s=L)

    cnt = {}

    def bump(instr, sem, n=1):
        instr.then_inc(sem, n)
        cnt[sem.name] = cnt.get(sem.name, 0) + n
        return cnt[sem.name]

    def val(sem):
        return cnt.get(sem.name, 0)

    def chain(e, sem):
        state = {'first': True}

        def emit(mk, snap=False):
            if not state['first']:
                e.drain()
            state['first'] = False
            instr = mk()
            if snap:
                return bump(instr, sem, 1)
            return None
        return emit

    import contextlib
    est = contextlib.ExitStack()
    with est:
        def S(name, shape, dtp):
            return est.enter_context(nc.sbuf_tensor(name, shape, dtp))

        def SEM(name):
            return est.enter_context(nc.semaphore(name))

        hallT = S("hallT", [N16, NT * 128], BF16)
        st_m = S("st_m", [128, NT], F32)
        st_r = S("st_r", [128, NT], F32)
        st_mr = S("st_mr", [128, NT], F32)
        st_s = S("st_s", [128, NT], F32)
        st_t = S("st_t", [128, NT], F32)
        st2_m = S("st2_m", [128, NT], F32)
        st2_r = S("st2_r", [128, NT], F32)
        st2_mr = S("st2_mr", [128, NT], F32)
        mc_s = S("mc_s", [P120, 256], F32)
        i128 = S("i128", [128, 128], BF16)
        if32 = S("if32", [128, 128], F32)
        i16s = S("i16s", [N16, N16], F32)
        knls = S("knls", [4, 128], F32)
        bsel = S("bsel_s", [2, 128], F32)
        bselb = S("bselb_s", [2, 128], BF16)
        esel = S("esel_s", [2, 32], F32)
        itls = S("itls", [128, 256], F32)
        cvals = S("cvals_s", [128, 2], F32)
        wpb16 = S("wpb16", [128, DS * P120], BF16)
        bwb16 = S("bwb16", [128, DS * N16], BF16)
        wpbT = S("wpbT", [P120, 1], F32)
        ctg = S("ctg", [N16, NB * D], BF16)
        gdca_t = S("gdca_t", [128, D], F32)
        gdsa = S("gdsa", [NB, D], BF16)
        cb1_bc = S("cb1_bc", [128, D], F32)
        sb_bc = S("sb_bc", [128, D], F32)
        gbsT = S("gbsT", [128, NB * DS], F32)
        eB = contextlib.ExitStack()
        wtok = eB.enter_context(nc.sbuf_tensor("wtok", [128, NT * 256], F32, side="right"))
        bxs = eB.enter_context(nc.sbuf_tensor("bxs", [128, NT * N16], F32, side="right"))
        hall = eB.enter_context(nc.sbuf_tensor("hall", [128, NT * N16], F32, side="right"))
        u0s = eB.enter_context(nc.sbuf_tensor("u0s", [128, NT * N16], F32, side="right"))
        ca1_bc = eB.enter_context(nc.sbuf_tensor("ca1_bc", [128, D], F32, side="right"))
        sa_bc = eB.enter_context(nc.sbuf_tensor("sa_bc", [128, D], F32, side="right"))
        sq = SEM("sq")
        gq = SEM("gq")
        tv = SEM("tv")
        vv = SEM("vv")
        ss = SEM("ss")
        gg = SEM("gg")
        so = SEM("so")
        gs = SEM("gs")
        bar = SEM("bar")
        blk = est.enter_context(nc.Block())
        nbar = [0]

        def barrier():
            nbar[0] += 5
            tgt = nbar[0]
            for name in ("vector", "scalar", "tensor", "gpsimd", "sync"):
                def _f(engine, tgt=tgt):
                    engine.drain().then_inc(bar, 1)
                    engine.wait_ge(bar, tgt)
                getattr(blk, name)(_f)

        wpr_done = [0]
        wfc_done = [0]
        e0 = contextlib.ExitStack()
        with e0:
            def S0(name, shape, dtp):
                return e0.enter_context(nc.sbuf_tensor(name, shape, dtp))

            e0buf = S0("e0buf", [128, 2 * 1536], F32)
            modsb = S0("modsb", [NB, 6 * D], F32)
            silu_c = S0("silu_c", [NB, D], F32)
            scT = S0("scT", [128, NB * DS], F32)
            mt0 = S0("mt0", [NB, D], F32)
            mt1 = S0("mt1", [NB, D], F32)
            mt2 = S0("mt2", [NB, D], F32)
            dsb = S0("dsb", [1, D], F32)
            wpraw = S0("wpraw", [128, DS * P120], F32)
            bwraw = S0("bwraw", [128, DS * N16], F32)
            ctraw = S0("ctraw", [N16, D], F32)
            psm0 = e0.enter_context(nc.psum_tensor("psm0", [NB, 1536], F32))
            pse0 = e0.enter_context(nc.psum_tensor("pse0", [128, 1024], F32))
            pse1 = e0.enter_context(nc.psum_tensor("pse1", [N16, D], F32))
            @blk.sync
            def _(e):
                for t_, s_ in ((mc_s, mc_d), (i128, i128b_d), (if32, i128f_d),
                               (i16s, i16_d), (knls, knl_d), (bsel, bsel_d), (bselb, bselb_d), (esel, esel_d),
                               (itls, itile_d), (u0s, u0_d), (cvals, cvals_d), (silu_c, cmod_d),
                               (ctraw, cT_d), (dsb, dv_d), (wpbT, wpb_d)):
                    bump(e.dma_start(out=t_[:], in_=s_), sq, 16)
                bump(e.dma_start(
                    out=wpraw[:].rearrange("p (s k) -> p s k", s=DS),
                    in_=wpT_d.rearrange("(s p) k -> p s k", p=128)), sq, 16)
                bump(e.dma_start(
                    out=bwraw[:].rearrange("p (s k) -> p s k", s=DS),
                    in_=bwT_d.rearrange("(s p) k -> p s k", p=128)), sq, 16)
            ld_small = val(sq)

            @blk.gpsimd
            def _(e):
                for s in range(2):
                    bump(e.dma_start(
                        out=wprB_d[s * 2048:(s + 1) * 2048, :],
                        in_=wprT_d[s * 2048:(s + 1) * 2048, :]), gq, 16)
            wpr_done[0] = val(gq)

            @blk.scalar
            def _(e):
                e.wait_ge(sq, ld_small)
                bump(e.activation(silu_c[:], silu_c[:], AF.Silu), ss, 1)
            silu_done = val(ss)

            @blk.vector
            def _(e):
                e.wait_ge(sq, ld_small)
                bump(e.tensor_copy(wpb16[:], wpraw[:]), vv, 1)
                bump(e.tensor_copy(bwb16[:], bwraw[:]), vv, 1)
            cast_done = val(vv)

            @blk.tensor
            def _(e):
                e.wait_ge(ss, silu_done)
                for s in range(DS):
                    bump(e.matmul(pse0[:, 2 * s:2 * s + 2],
                                  silu_c[:, s * 128:(s + 1) * 128],
                                  i16s[0:2, 0:2], start=True, stop=True), tv, 1)
            scT_mm = val(tv)

            @blk.vector
            def _(e):
                e.wait_ge(tv, scT_mm)
                bump(e.tensor_copy(scT[:], pse0[:, 0:2 * DS]), vv, 1)
            scT_done = val(vv)

            @blk.sync
            def _(e):
                for h in range(4):
                    for s in range(DS):
                        i = h * DS + s
                        slot = i % 2
                        if i >= 2:
                            e.wait_ge(tv, scT_mm + (i - 1) * 3)
                        bump(e.dma_start(
                            out=e0buf[:, slot * 1536:(slot + 1) * 1536],
                            in_=wmod_d[s * 128:(s + 1) * 128,
                                       h * 1536:(h + 1) * 1536]), sq, 16)
            wmod_ld = val(sq)

            mod_ev = [0, 0, 0, 0]
            for h in range(4):
                @blk.tensor
                def _(e, h=h):
                    e.wait_ge(vv, scT_done)
                    if h >= 1:
                        e.wait_ge(vv, mod_ev[h - 1])
                    for s in range(DS):
                        i = h * DS + s
                        slot = i % 2
                        e.wait_ge(sq, wmod_ld - (31 - i) * 16)
                        for n in range(3):
                            bump(e.matmul(
                                psm0[:, n * 512:(n + 1) * 512],
                                scT[:, 2 * s:2 * s + 2],
                                e0buf[:, slot * 1536 + n * 512:
                                      slot * 1536 + (n + 1) * 512],
                                start=(s == 0), stop=(s == DS - 1)), tv, 1)
                mmh = val(tv)

                @blk.vector
                def _(e, h=h, mmh=mmh):
                    e.wait_ge(tv, mmh)
                    mod_ev[h] = bump(e.tensor_copy(
                        modsb[:, h * 1536:(h + 1) * 1536], psm0[:, 0:1536]), vv, 1)

            SA = slice(0, D); CA = slice(D, 2 * D); GA = slice(2 * D, 3 * D)
            SB_ = slice(3 * D, 4 * D); CB = slice(4 * D, 5 * D)
            GB = slice(5 * D, 6 * D)

            @blk.vector
            def _(e):
                e.wait_ge(vv, mod_ev[3])
                c = chain(e, vv)
                for gsl in (GA, GB):
                    c(lambda gsl=gsl: e.tensor_single_scalar(
                        mt0[:], modsb[:, gsl], 0.0, op=OP.is_ge))
                    c(lambda: e.tensor_scalar(mt0[:], mt0[:], 2.0, -1.0,
                                              op0=OP.mult, op1=OP.add))
                    c(lambda gsl=gsl: e.tensor_mul(mt1[:], modsb[:, gsl],
                                                   mt0[:]))
                    c(lambda: e.tensor_scalar_max(mt1[:], mt1[:], 0.25))
                    c(lambda gsl=gsl: e.tensor_mul(modsb[:, gsl], mt1[:],
                                                   mt0[:]))
                c(lambda: e.tensor_scalar_add(modsb[:, CA], modsb[:, CA], 1.0))
                c(lambda: e.tensor_scalar_add(modsb[:, CB], modsb[:, CB], 1.0),
                  snap=True)
            clamp_done = val(vv)

            @blk.tensor
            def _(e):
                e.wait_ge(vv, clamp_done)
                for n in range(2):
                    bump(e.matmul(pse1[0:2, n * 512:(n + 1) * 512],
                                  knls[0:1, 0:2], dsb[:, n * 512:(n + 1) * 512],
                                  start=True, stop=True), tv, 1)
            dexp_mm = val(tv)

            @blk.vector
            def _(e):
                e.wait_ge(tv, dexp_mm)
                c = chain(e, vv)
                c(lambda: e.tensor_mul(mt2[:], modsb[:, GA], pse1[0:2, :]))
                c(lambda: e.tensor_mul(gdsa[:], mt2[:], modsb[:, SA]))
                c(lambda: e.tensor_mul(mt1[:], mt2[:], modsb[:, CA]))
                c(lambda: e.tensor_scalar_mul(mt0[:], modsb[:, GB], 0.2),
                  snap=True)
            drows_done = val(vv)

            @blk.tensor
            def _(e):
                e.wait_ge(vv, drows_done)
                for s in range(DS):
                    bump(e.matmul(pse0[:, 2 * s:2 * s + 2],
                                  mt0[:, s * 128:(s + 1) * 128],
                                  i16s[0:2, 0:2], start=True, stop=True), tv, 1)
            gbs_mm = val(tv)

            @blk.vector
            def _(e):
                e.wait_ge(tv, gbs_mm)
                bump(e.tensor_copy(gbsT[:], pse0[:, 0:2 * DS]), vv, 1)
            gbs_done = val(vv)

            ctg_done = [0]
            for b in range(NB):
                @blk.tensor
                def _(e, b=b):
                    e.wait_ge(vv, gbs_done if b == 0 else ctg_done[0])
                    for n in range(2):
                        bump(e.matmul(pse1[:, n * 512:(n + 1) * 512],
                                      esel[:, b * N16:(b + 1) * N16],
                                      modsb[0:2, GA][:, n * 512:(n + 1) * 512],
                                      start=True, stop=True), tv, 1)
                cmm = val(tv)

                @blk.vector
                def _(e, b=b, cmm=cmm):
                    e.wait_ge(tv, cmm)
                    ctg_done[0] = bump(e.tensor_mul(
                        ctg[:, b * D:(b + 1) * D], ctraw[:], pse1[:]), vv, 1)

            bspec = [(ca1_bc, CA, None), (sa_bc, SA, None), (cb1_bc, CB, None),
                     (sb_bc, SB_, None), (gdca_t, None, mt1)]
            bct_prev = [ctg_done[0]]
            for ti, (dst, rsl, alt) in enumerate(bspec):
                @blk.tensor
                def _(e, rsl=rsl, alt=alt):
                    e.wait_ge(vv, bct_prev[0])
                    for n in range(2):
                        rows = (modsb[0:2, rsl] if alt is None
                                else alt[0:2, :])[:, n * 512:(n + 1) * 512]
                        bump(e.matmul(pse0[:, n * 512:(n + 1) * 512],
                                      bsel[:], rows, start=True, stop=True),
                             tv, 1)
                bmm = val(tv)

                @blk.vector
                def _(e, dst=dst, bmm=bmm):
                    e.wait_ge(tv, bmm)
                    bct_prev[0] = bump(e.tensor_copy(dst[:], pse0[:]), vv, 1)

            if dbg:
                @blk.gpsimd
                def _(e):
                    e.wait_ge(vv, bct_prev[0])
                    bump(e.dma_start(out=dbg_mod, in_=modsb[:]), gg, 16)
                    bump(e.dma_start(out=dbg_bc[:, 0:D], in_=ca1_bc[:]), gg, 16)
                    bump(e.dma_start(out=dbg_bc[:, D:2 * D], in_=sa_bc[:]), gg, 16)

        barrier()

        e1 = contextlib.ExitStack()
        with e1:
            def S1(name, shape, dtp):
                return e1.enter_context(nc.sbuf_tensor(name, shape, dtp))

            xbuf = S1("xbuf", [128, 2 * D], F32)
            sqb = S1("sqb", [128, D], F32)
            xnb = S1("xnb", [128, D], BF16)
            htk = S1("htk", [128, D], BF16)
            hdm = S1("hdm", [128, DS * 512], BF16)
            vsb = S1("vsb", [P120, 512], F32)
            bxd = S1("bxd", [N16, 512], F32)
            atok = S1("atok", [128, 4 * 256], F32)
            a2b = S1("a2b", [128, 256], F32)
            a3b = S1("a3b", [128, 256], F32)
            a4b = S1("a4b", [128, 256], F32)
            wcmb = S1("wcmb", [128, 256], F32)
            wcm2 = S1("wcm2", [128, 256], F32)
            tmp4k = S1("tmp4k", [128, 4096], F32)
            piv = S1("piv", [128, 256], F32)
            pisq = S1("pisq", [128, 64], F32)
            psv = e1.enter_context(nc.psum_tensor("psv", [P120, 512], F32))
            psbx = e1.enter_context(nc.psum_tensor("psbx", [N16, 512], F32))
            psbt = e1.enter_context(nc.psum_tensor("psbt", [128, 64], F32))
            psa = e1.enter_context(nc.psum_tensor("psa", [128, 4 * 256], F32))
            pstr = e1.enter_context(nc.psum_tensor("pstr", [128, 1024], F32))
            xld = [0] * NT
            sqd = [0] * NT
            xnd = [0] * NT
            sdd = [0] * NT
            var_d = [0] * NT
            rch = [0] * NT
            hdone = [0] * NT
            httr = [0] * NT
            hev = [0] * NT
            ssq_red = [0] * NT
            piq = [0] * NT
            pisd = [0] * NT
            aev = [0] * NT
            wdn = [0] * NT
            vev = [0] * 8
            bxe = [0] * 8
            vmm = [0] * 8
            bxmm = [0] * 8
            bxtr = [0] * 8
            amm = [0] * 8
            bxs_ev = [0] * 8

            for ch in range(8):
                for tau in range(4):
                    g = 4 * ch + tau
                    sl = g % 2

                    @blk.sync
                    def _(e, g=g, sl=sl):
                        if g >= 2:
                            e.wait_ge(vv, xnd[g - 2])
                        xld[g] = bump(e.dma_start(
                            out=xbuf[:, sl * D:(sl + 1) * D], in_=xr[g]), sq, 16)

                    @blk.scalar
                    def _(e, g=g, sl=sl):
                        e.wait_ge(sq, xld[g])
                        if g >= 1:
                            e.wait_ge(vv, ssq_red[g - 1])
                        sqd[g] = bump(e.activation(
                            sqb[:],
                            xbuf[:, sl * D:(sl + 1) * D], AF.Square), ss, 1)

                    @blk.vector
                    def _(e, g=g, sl=sl):
                        xt = xbuf[:, sl * D:(sl + 1) * D]
                        e.wait_ge(sq, xld[g])
                        c = chain(e, vv)
                        c(lambda: e.reduce_sum(st_m[:, g:g + 1], xt, axis=AX.X))
                        e.wait_ge(ss, sqd[g])
                        ssq_red[g] = c(lambda: e.reduce_sum(
                            st_t[:, g:g + 1], sqb[:], axis=AX.X), snap=True)
                        c(lambda: e.tensor_scalar_mul(
                            st_m[:, g:g + 1], st_m[:, g:g + 1], 1.0 / D))
                        c(lambda: e.tensor_scalar_mul(
                            st_t[:, g:g + 1], st_t[:, g:g + 1], 1.0 / D))
                        c(lambda: e.tensor_mul(st_s[:, g:g + 1],
                                               st_m[:, g:g + 1],
                                               st_m[:, g:g + 1]))
                        c(lambda: e.tensor_sub(st_t[:, g:g + 1],
                                               st_t[:, g:g + 1],
                                               st_s[:, g:g + 1]))
                        var_d[g] = c(lambda: e.tensor_scalar_add(
                            st_t[:, g:g + 1], st_t[:, g:g + 1], 1e-5), snap=True)

                    if dbg and g == 0:
                        @blk.gpsimd
                        def _(e):
                            e.wait_ge(vv, var_d[0])
                            with nc.allow_non_contiguous_dma(reason="dbg"):
                                bump(e.dma_start(out=dbg_st0[:, 3:4],
                                                 in_=st_t[:, 0:1]), gg, 16)
                        dbg_vd = val(gg)

                    @blk.scalar
                    def _(e, g=g):
                        e.wait_ge(vv, var_d[g])
                        if dbg and g == 0:
                            e.wait_ge(gg, dbg_vd)
                        e.activation(st_r[:, g:g + 1], st_t[:, g:g + 1], AF.Ln)
                        e.drain()
                        sdd[g] = bump(e.activation(
                            st_r[:, g:g + 1], st_r[:, g:g + 1], AF.Exp,
                            scale=cvals[:, 0:1]), ss, 1)

                    if dbg and g == 0:
                        @blk.gpsimd
                        def _(e):
                            e.wait_ge(ss, sdd[0])
                            with nc.allow_non_contiguous_dma(reason="dbg"):
                                bump(e.dma_start(out=dbg_st0[:, 4:5],
                                                 in_=st_r[:, 0:1]), gg, 16)
                        dbg_sd = val(gg)

                    @blk.vector
                    def _(e, g=g):
                        e.wait_ge(ss, sdd[g])
                        if dbg and g == 0:
                            e.wait_ge(gg, dbg_sd)
                        rch[g] = bump(e.scalar_tensor_tensor(
                            st_mr[:, g:g + 1], st_m[:, g:g + 1], -1.0,
                            st_r[:, g:g + 1], op0=OP.mult, op1=OP.mult), vv, 1)

                    @blk.vector
                    def _(e, g=g, sl=sl):
                        if g >= 1:
                            e.wait_ge(tv, httr[g - 1])
                        e.wait_ge(vv, rch[g])
                        c = chain(e, vv)
                        c(lambda: e.tensor_tensor(
                            xnb[:], xbuf[:, sl * D:(sl + 1) * D],
                            st_r[:, g:g + 1].broadcast_to([128, D]),
                            op=OP.mult))
                        xnd[g] = c(lambda: e.tensor_tensor(
                            xnb[:], xnb[:],
                            st_mr[:, g:g + 1].broadcast_to([128, D]),
                            op=OP.add), snap=True)

                    @blk.vector
                    def _(e, g=g):
                        e.wait_ge(vv, xnd[g])
                        c = chain(e, vv)
                        c(lambda: e.tensor_mul(htk[:], xnb[:], ca1_bc[:]))
                        hdone[g] = c(lambda: e.tensor_add(
                            htk[:], htk[:], sa_bc[:]), snap=True)

                    if dbg and g == 0:
                        @blk.gpsimd
                        def _(e):
                            e.wait_ge(vv, hdone[0])
                            bump(e.dma_start(out=dbg_xn, in_=xnb[:]), gg, 16)
                            bump(e.dma_start(out=dbg_htk, in_=htk[:]), gg, 16)
                            with nc.allow_non_contiguous_dma(reason="dbg"):
                                bump(e.dma_start(out=dbg_st0[:, 0:1], in_=st_m[:, 0:1]), gg, 16)
                                bump(e.dma_start(out=dbg_st0[:, 1:2], in_=st_r[:, 0:1]), gg, 16)
                                bump(e.dma_start(out=dbg_st0[:, 2:3], in_=st_mr[:, 0:1]), gg, 16)
                        dbg_g0 = val(gg)

                    if dbg and g == 1:
                        @blk.vector
                        def _(e):
                            e.wait_ge(gg, dbg_g0)

                    @blk.tensor
                    def _(e, g=g):
                        e.wait_ge(vv, hdone[g])
                        if g >= 1:
                            e.wait_ge(vv, hev[g - 1])
                        for sdx in range(DS):
                            httr[g] = bump(e.matmul(
                                pstr[:, sdx * 128:(sdx + 1) * 128],
                                htk[:, sdx * 128:(sdx + 1) * 128],
                                i128[:], start=True, stop=True), tv, 1)

                    @blk.vector
                    def _(e, g=g, tau=tau, ch=ch):
                        e.wait_ge(tv, httr[g])
                        if ch > 0 and tau == 0:
                            e.wait_ge(tv, bxmm[ch - 1])
                        hev[g] = bump(e.tensor_copy(
                            hdm[:].rearrange("p (s t) -> p s t",
                                             s=DS)[:, :, tau * 128:(tau + 1) * 128],
                            pstr[:].rearrange("p (s t) -> p s t", s=DS)), vv, 1)

                @blk.tensor
                def _(e, ch=ch):
                    e.wait_ge(vv, hev[4 * ch + 3])
                    if ch > 0:
                        e.wait_ge(vv, vev[ch - 1])
                    for k in range(DS):
                        vmm[ch] = bump(e.matmul(
                            psv[:], wpb16[:, k * P120:(k + 1) * P120],
                            hdm[:, k * 512:(k + 1) * 512],
                            start=(k == 0), stop=(k == DS - 1)), tv, 1)
                    if ch > 0:
                        e.wait_ge(vv, bxe[ch - 1])
                    for k in range(DS):
                        bxmm[ch] = bump(e.matmul(
                            psbx[:], bwb16[:, k * N16:(k + 1) * N16],
                            hdm[:, k * 512:(k + 1) * 512],
                            start=(k == 0), stop=(k == DS - 1)), tv, 1)

                @blk.vector
                def _(e, ch=ch):
                    e.wait_ge(tv, vmm[ch])
                    vev[ch] = bump(e.tensor_scalar_add(vsb[:], psv[:], wpbT[:]),
                                   vv, 1)
                    e.wait_ge(tv, bxmm[ch])
                    bxe[ch] = bump(e.tensor_copy(bxd[:], psbx[:]), vv, 1)

                @blk.tensor
                def _(e, ch=ch):
                    e.wait_ge(vv, bxe[ch])
                    if ch > 0:
                        e.wait_ge(ss, bxs_ev[ch - 1])
                    for tau in range(4):
                        bxtr[ch] = bump(e.matmul(
                            psbt[:, tau * N16:(tau + 1) * N16],
                            bxd[:, tau * 128:(tau + 1) * 128], i16s[:],
                            start=True, stop=True), tv, 1)
                    e.wait_ge(vv, vev[ch])
                    if ch > 0:
                        e.wait_ge(vv, aev[4 * ch - 1])
                    for tau in range(4):
                        amm[ch] = bump(e.matmul(
                            psa[:, tau * 256:(tau + 1) * 256],
                            vsb[:, tau * 128:(tau + 1) * 128], mc_s[:],
                            start=True, stop=True), tv, 1)

                @blk.scalar
                def _(e, ch=ch):
                    e.wait_ge(tv, bxtr[ch])
                    bxs_ev[ch] = bump(e.copy(
                        bxs[:, 4 * ch * N16:(4 * ch + 4) * N16], psbt[:]), ss, 1)

                @blk.vector
                def _(e, ch=ch):
                    e.wait_ge(tv, amm[ch])
                    for tau in range(4):
                        g = 4 * ch + tau
                        aev[g] = bump(e.tensor_copy(
                            atok[:, tau * 256:(tau + 1) * 256],
                            psa[:, tau * 256:(tau + 1) * 256]), vv, 1)

                    def AV(tau):
                        return atok[:, tau * 256:(tau + 1) * 256].rearrange(
                            "p (i j) -> p i j", i=16)

                    def T4(tau):
                        return tmp4k[:, tau * 256:(tau + 1) * 256]

                    def stage(mk):
                        for tau in range(4):
                            mk(tau, 4 * ch + tau)

                    B16 = [128, 16, 16]
                    stage(lambda tau, g: e.tensor_mul(
                        T4(tau).rearrange("p (i j) -> p i j", i=16), AV(tau),
                        u0s[:, g * N16:(g + 1) * N16].unsqueeze(2).broadcast_to(B16)))
                    stage(lambda tau, g: e.tensor_reduce(
                        piv[:, tau * 64:tau * 64 + 16],
                        T4(tau).rearrange("p (i j) -> p j i", i=16),
                        axis=AX.X, op=OP.add))
                    stage(lambda tau, g: e.tensor_mul(
                        T4(tau).rearrange("p (i j) -> p i j", i=16), AV(tau),
                        piv[:, tau * 64:tau * 64 + 16].unsqueeze(1).broadcast_to(B16)))
                    stage(lambda tau, g: e.tensor_reduce(
                        piv[:, tau * 64 + 16:tau * 64 + 32],
                        T4(tau).rearrange("p (i j) -> p i j", i=16),
                        axis=AX.X, op=OP.add))
                    stage(lambda tau, g: e.tensor_mul(
                        T4(tau).rearrange("p (i j) -> p i j", i=16), AV(tau),
                        piv[:, tau * 64 + 16:tau * 64 + 32].unsqueeze(2).broadcast_to(B16)))
                    stage(lambda tau, g: e.tensor_reduce(
                        piv[:, tau * 64 + 32:tau * 64 + 48],
                        T4(tau).rearrange("p (i j) -> p j i", i=16),
                        axis=AX.X, op=OP.add))
                    stage(lambda tau, g: e.tensor_mul(
                        T4(tau).rearrange("p (i j) -> p i j", i=16), AV(tau),
                        piv[:, tau * 64 + 32:tau * 64 + 48].unsqueeze(1).broadcast_to(B16)))
                    stage(lambda tau, g: e.tensor_reduce(
                        piv[:, tau * 64 + 48:tau * 64 + 64],
                        T4(tau).rearrange("p (i j) -> p i j", i=16),
                        axis=AX.X, op=OP.add))
                    stage(lambda tau, g: e.tensor_mul(
                        pisq[:, tau * 16:(tau + 1) * 16],
                        piv[:, tau * 64 + 32:tau * 64 + 48],
                        piv[:, tau * 64 + 32:tau * 64 + 48]))
                    stage(lambda tau, g: e.reduce_sum(
                        st_s[:, g:g + 1], pisq[:, tau * 16:(tau + 1) * 16],
                        axis=AX.X))
                    stage(lambda tau, g: e.tensor_mul(
                        pisq[:, tau * 16:(tau + 1) * 16],
                        piv[:, tau * 64 + 48:tau * 64 + 64],
                        piv[:, tau * 64 + 48:tau * 64 + 64]))
                    stage(lambda tau, g: e.reduce_sum(
                        st_t[:, g:g + 1], pisq[:, tau * 16:(tau + 1) * 16],
                        axis=AX.X))
                    stage(lambda tau, g: e.tensor_scalar_max(
                        st_t[:, g:g + 1], st_t[:, g:g + 1], 1e-30))
                    stage(lambda tau, g: e.reciprocal(
                        st_t[:, g:g + 1], st_t[:, g:g + 1]))
                    stage(lambda tau, g: e.scalar_tensor_tensor(
                        st_s[:, g:g + 1], st_s[:, g:g + 1], 0.09,
                        st_t[:, g:g + 1], op0=OP.mult, op1=OP.mult))
                    pv_ = bump(e.drain(), vv, 1)
                    for tau in range(4):
                        piq[4 * ch + tau] = pv_

                @blk.scalar
                def _(e, ch=ch):
                    e.wait_ge(vv, piq[4 * ch])
                    for tau in range(4):
                        g = 4 * ch + tau
                        e.activation(st_s[:, g:g + 1], st_s[:, g:g + 1], AF.Ln)
                    e.drain()
                    pd_ = 0
                    for tau in range(4):
                        g = 4 * ch + tau
                        pd_ = bump(e.activation(
                            st_s[:, g:g + 1], st_s[:, g:g + 1], AF.Exp,
                            scale=cvals[:, 1:2]), ss, 1)
                    for tau in range(4):
                        pisd[4 * ch + tau] = pd_

                for tau in range(4):
                    g = 4 * ch + tau

                    @blk.vector
                    def _(e, g=g, tau=tau):
                        A = atok[:, tau * 256:(tau + 1) * 256]
                        e.wait_ge(ss, pisd[g])
                        c = chain(e, vv)
                        c(lambda: e.tensor_scalar_min(
                            st_s[:, g:g + 1], st_s[:, g:g + 1], 1.0))
                        if dbg and g == 0:
                            e.tensor_copy(piv[:, 0:1], st_s[:, 0:1])
                        c(lambda: e.tensor_tensor(
                            A, A, st_s[:, g:g + 1].broadcast_to([128, 256]),
                            op=OP.mult))
                        if dbg and g == 0:
                            bump(e.tensor_copy(piv[:, 1:2], st_s[:, 0:1]), vv, 1)
                            dbg_smin = val(vv)

                            @blk.gpsimd
                            def _(e2):
                                e2.wait_ge(vv, dbg_smin)
                                with nc.allow_non_contiguous_dma(reason="dbg"):
                                    bump(e2.dma_start(out=dbg_pi[:, 66:67],
                                                      in_=piv[:, 0:1]), gg, 16)
                        t4 = tmp4k[:].rearrange("p (i j c) -> p i j c",
                                                i=16, j=16)
                        t4r = tmp4k[:].rearrange("p (i j c) -> p i c j",
                                                 i=16, j=16)

                        def mm16(dst, lf, rf):
                            c(lambda: e.tensor_tensor(
                                t4,
                                lf.rearrange("p (i j) -> p i j",
                                             i=16).unsqueeze(3).broadcast_to(
                                    [128, 16, 16, 16]),
                                rf.rearrange("p (j c) -> p j c",
                                             j=16).unsqueeze(1).broadcast_to(
                                    [128, 16, 16, 16]),
                                op=OP.mult))
                            return c(lambda: e.tensor_reduce(
                                dst.rearrange("p (i c) -> p i c", i=16),
                                t4r, axis=AX.X, op=OP.add))

                        mm16(a2b[:], A, A)
                        mm16(a3b[:], A, a2b[:])
                        mm16(a4b[:], a2b[:], a2b[:])
                        c(lambda: e.scalar_tensor_tensor(
                            wcmb[:], A, -2.0, itls[:],
                            op0=OP.mult, op1=OP.add))
                        c(lambda: e.scalar_tensor_tensor(
                            wcm2[:], a2b[:], 2.0, wcmb[:],
                            op0=OP.mult, op1=OP.add))
                        c(lambda: e.scalar_tensor_tensor(
                            wcmb[:], a3b[:], -2.0, wcm2[:],
                            op0=OP.mult, op1=OP.add))
                        wdn[g] = c(lambda: e.scalar_tensor_tensor(
                            wtok[:, g * 256:(g + 1) * 256], a4b[:], 1.0, wcmb[:],
                            op0=OP.mult, op1=OP.add), snap=True)

                if dbg and ch == 0:
                    @blk.gpsimd
                    def _(e):
                        e.wait_ge(vv, hev[3])
                        bump(e.dma_start(out=dbg_hdm, in_=hdm[:]), gg, 16)
                        e.wait_ge(vv, aev[3])
                        bump(e.dma_start(out=dbg_a, in_=atok[:]), gg, 16)
                        e.wait_ge(vv, vev[0])
                        bump(e.dma_start(out=dbg_v, in_=vsb[:]), gg, 16)
                        e.wait_ge(vv, wdn[3])
                        bump(e.dma_start(out=dbg_st[:, 0:NT], in_=st_m[:]), gg, 16)
                        bump(e.dma_start(out=dbg_st[:, NT:2 * NT], in_=st_r[:]), gg, 16)
                        bump(e.dma_start(out=dbg_st[:, 2 * NT:3 * NT], in_=st_mr[:]), gg, 16)
                        bump(e.dma_start(out=dbg_st[:, 3 * NT:4 * NT], in_=st_s[:]), gg, 16)

        barrier()

        eC = contextlib.ExitStack()
        wfcsb = eC.enter_context(
            nc.sbuf_tensor("wfcsb", [128, DS * 4 * D], BF16))

        @blk.gpsimd
        def _(e):
            for s in range(DS):
                wfc_done[0] = bump(
                    e.dma_start(out=wfcsb[:, s * 4 * D:(s + 1) * 4 * D],
                                in_=wfcT_d[s * 128:(s + 1) * 128, :]), gq, 16)

        e2 = contextlib.ExitStack()
        with e2:
            def S2(name, shape, dtp):
                return e2.enter_context(nc.sbuf_tensor(name, shape, dtp))

            hs = S2("hs", [128, N16], F32)
            hred = S2("hred", [128, N16], F32)
            tmp256 = S2("tmp256", [128, 256], F32)
            tmp4k2 = S2("tmp4k2", [128, 4096], F32)
            pmJ = S2("pmJ", [128, 256], F32)
            pmm = S2("pmm", [N16, 128 * N16], F32)
            hlocT = S2("hlocT", [N16, 128], F32)
            hbS = S2("hbS", [N16, 128], F32)
            hbSh = S2("hbSh", [N16, 128], F32)
            hinit = S2("hinit", [128, N16], F32)
            ps_s1 = e2.enter_context(nc.psum_tensor("ps_s1", [N16, 128], F32))
            ps_s2 = e2.enter_context(nc.psum_tensor("ps_s2", [128, N16], F32))
            psq2 = e2.enter_context(nc.psum_tensor("psq2", [N16, 2], F32))
            @blk.vector
            def _(e):
                c = chain(e, vv)
                c(lambda: e.tensor_copy(hs[:], bxs[:, 0:N16]))
                c(lambda: e.tensor_copy(
                    pmJ[:].rearrange("p (j i) -> p j i", j=16),
                    wtok[:, 0:256].rearrange("p (i j) -> p j i", i=16)))
                for s in range(1, L):
                    Wv = wtok[:, s * 256:(s + 1) * 256].rearrange(
                        "p (i j) -> p i j", i=16)
                    c(lambda Wv=Wv: e.tensor_mul(
                        tmp256[:].rearrange("p (i j) -> p i j", i=16), Wv,
                        hs[:].unsqueeze(1).broadcast_to([128, 16, 16])))
                    c(lambda: e.tensor_reduce(
                        hred[:], tmp256[:].rearrange("p (i j) -> p i j", i=16),
                        axis=AX.X, op=OP.add))
                    c(lambda s=s: e.tensor_add(
                        hs[:], hred[:], bxs[:, s * N16:(s + 1) * N16]))
                    c(lambda Wv=Wv: e.tensor_tensor(
                        tmp4k2[:].rearrange("p (i j c) -> p i j c", i=16, j=16),
                        Wv.unsqueeze(3).broadcast_to([128, 16, 16, 16]),
                        pmJ[:].rearrange("p (c j) -> p j c",
                                         c=16).unsqueeze(1).broadcast_to(
                            [128, 16, 16, 16]),
                        op=OP.mult))
                    c(lambda: e.tensor_reduce(
                        pmJ[:].rearrange("p (c i) -> p i c", c=16),
                        tmp4k2[:].rearrange("p (i j c) -> p i c j", i=16, j=16),
                        axis=AX.X, op=OP.add))
                bump(e.drain(), vv, 1)
            p1v = val(vv)

            @blk.gpsimd
            def _(e):
                e.wait_ge(vv, p1v)
                for j in range(N16):
                    bump(e.dma_start(out=pmm[j:j + 1, :],
                                     in_=pmJ[:, 16 * j:16 * (j + 1)]), gq, 16)
            pmm_done = val(gq)

            @blk.tensor
            def _(e):
                e.wait_ge(vv, p1v)
                bump(e.matmul(ps_s1[:], hs[:], if32[:], start=True, stop=True),
                     tv, 1)
            hlt_mm = val(tv)

            @blk.vector
            def _(e):
                e.wait_ge(tv, hlt_mm)
                c = chain(e, vv)
                c(lambda: e.tensor_copy(hlocT[:], ps_s1[:]))
                c(lambda: e.tensor_copy(
                    hbS[:].rearrange("p (b c) -> p b c", b=2)[:, :, 0:1],
                    hlocT[:].rearrange("p (b c) -> p b c", b=2)[:, :, 0:1]))
            hbup = [0] * CCH
            hbup[0] = val(vv)

            for c in range(1, CCH):
                @blk.tensor
                def _(e, c=c):
                    e.wait_ge(vv, hbup[c - 1])
                    if c == 1:
                        e.wait_ge(gq, pmm_done)
                    for b in range(NB):
                        inst = b * CCH + c
                        bump(e.matmul(psq2[:, b:b + 1],
                                      pmm[:, inst * N16:(inst + 1) * N16],
                                      hbS[:, b * CCH + c - 1:b * CCH + c],
                                      start=True, stop=True), tv, 1)
                mmv = val(tv)

                @blk.vector
                def _(e, c=c, mmv=mmv):
                    e.wait_ge(tv, mmv)
                    hbup[c] = bump(e.tensor_add(
                        hbS[:].rearrange("p (b c) -> p b c", b=2)[:, :, c:c + 1],
                        psq2[:].unsqueeze(2),
                        hlocT[:].rearrange("p (b c) -> p b c",
                                           b=2)[:, :, c:c + 1]), vv, 1)

            @blk.vector
            def _(e):
                e.wait_ge(vv, hbup[CCH - 1])
                c = chain(e, vv)
                c(lambda: e.memset(hbSh[:], 0.0))
                c(lambda: e.tensor_copy(hbSh[:, 1:CCH], hbS[:, 0:CCH - 1]))
                c(lambda: e.tensor_copy(hbSh[:, CCH + 1:2 * CCH],
                                        hbS[:, CCH:2 * CCH - 1]), snap=True)
            hsh_v = val(vv)

            @blk.tensor
            def _(e):
                e.wait_ge(vv, hsh_v)
                bump(e.matmul(ps_s2[:], hbSh[:], i16s[:], start=True, stop=True),
                     tv, 1)
            hin_mm = val(tv)

            hin_v_box = [0]

            @blk.vector
            def _(e):
                e.wait_ge(tv, hin_mm)
                c = chain(e, vv)
                hin_v_box[0] = c(lambda: e.tensor_copy(hinit[:], ps_s2[:]),
                                 snap=True)
                for s in range(L):
                    Wv = wtok[:, s * 256:(s + 1) * 256].rearrange(
                        "p (i j) -> p i j", i=16)
                    prev = hinit[:] if s == 0 else hall[:, (s - 1) * N16:s * N16]
                    c(lambda Wv=Wv, prev=prev: e.tensor_mul(
                        tmp256[:].rearrange("p (i j) -> p i j", i=16), Wv,
                        prev.unsqueeze(1).broadcast_to([128, 16, 16])))
                    c(lambda: e.tensor_reduce(
                        hred[:], tmp256[:].rearrange("p (i j) -> p i j", i=16),
                        axis=AX.X, op=OP.add))
                    c(lambda s=s: e.tensor_add(
                        hall[:, s * N16:(s + 1) * N16], hred[:],
                        bxs[:, s * N16:(s + 1) * N16]))
                bump(e.drain(), vv, 1)
            p3v = val(vv)

            htl = [0] * NT
            hte = [0] * NT
            for g in range(NT):
                @blk.tensor
                def _(e, g=g):
                    if g == 0:
                        e.wait_ge(vv, p3v)
                    else:
                        e.wait_ge(vv, hte[g - 1])
                    htl[g] = bump(e.matmul(
                        ps_s1[:], hall[:, g * N16:(g + 1) * N16], if32[:],
                        start=True, stop=True), tv, 1)

                @blk.vector
                def _(e, g=g):
                    e.wait_ge(tv, htl[g])
                    hte[g] = bump(e.tensor_copy(
                        hallT[:, g * 128:(g + 1) * 128], ps_s1[:]), vv, 1)

            if dbg:
                @blk.gpsimd
                def _(e):
                    e.wait_ge(vv, p1v)
                    bump(e.dma_start(out=dbg_sc2[:, 0:256], in_=pmJ[:]), gg, 16)
                    bump(e.dma_start(out=dbg_sc2[:, 256:272], in_=hs[:]), gg, 16)
                    e.wait_ge(vv, hin_v_box[0])
                    bump(e.dma_start(out=dbg_sc2[:, 272:288], in_=hinit[:]), gg, 16)
                    bump(e.dma_start(out=dbg_hb[:, 0:128], in_=hlocT[:]), gg, 16)
                    bump(e.dma_start(out=dbg_hb[:, 128:256], in_=hbS[:]), gg, 16)
                    bump(e.dma_start(out=dbg_hb[:, 256:384], in_=hbSh[:]), gg, 16)
                    e.wait_ge(vv, hte[NT - 1])
                    bump(e.dma_start(out=dbg_w, in_=wtok[:]), gg, 16)
                    bump(e.dma_start(out=dbg_hall, in_=hall[:]), gg, 16)
                    bump(e.dma_start(out=dbg_bxs, in_=bxs[:]), gg, 16)
                    bump(e.dma_start(out=dbg_sc, in_=st_s[:]), gg, 16)

        barrier()
        eB.close()

        e3 = contextlib.ExitStack()
        with e3:
            def S3(name, shape, dtp):
                return e3.enter_context(nc.sbuf_tensor(name, shape, dtp))

            x2b = S3("x2b", [128, TCH2 * D], F32)
            xn2b = S3("xn2b", [128, D], BF16)
            t2a = S3("t2a", [128, D], F32)
            xmid = S3("xmid", [128, 3 * TCH2 * D], F32)
            ycp = S3("ycp", [128, DS * NTK], F32)
            h2t = S3("h2t", [128, TCH2 * D], BF16)
            x2d = S3("x2d", [128, 2 * DS * NTK], BF16)
            actb = S3("actb", [128, 8 * NTK], BF16)
            relb = S3("relb", [128, 2 * NTK], BF16)
            memb = S3("memb", [128, DS * NTK], F32)
            memc = S3("memc", [128, DS * NTK], F32)
            sp1 = S3("sp1", [128, DS * NTK], BF16)
            sp2 = S3("sp2", [128, DS * NTK], BF16)
            spka = S3("spka", [128, 2 * DS * NTK], BF16)
            spkt = S3("spkt", [128, TCH2 * D], BF16)
            wprb = S3("wprb", [128, 4 * D], BF16)
            ps2t = e3.enter_context(nc.psum_tensor("ps2t", [128, D], F32))
            psy1 = ps2t
            pstr2 = ps2t
            psfc = e3.enter_context(nc.psum_tensor("psfc", [128, 4 * 256], F32))
            pspj = e3.enter_context(nc.psum_tensor("pspj", [128, DS * NTK], F32))
            x2ld = [0] * NCH2
            y1mm = [[0] * TCH2 for _ in range(NCH2)]
            xmd = [[0] * TCH2 for _ in range(NCH2)]
            h2dn = [[0] * TCH2 for _ in range(NCH2)]
            h2tr = [[0] * TCH2 for _ in range(NCH2)]
            x2dev = [[0] * TCH2 for _ in range(NCH2)]
            fcmm = [[0] * FS for _ in range(NCH2)]
            reld = [[0] * FS for _ in range(NCH2)]
            wld = [[0] * FS for _ in range(NCH2)]
            pjmm = [[0] * FS for _ in range(NCH2)]
            spdone = [0] * NCH2
            sptr = [[0] * TCH2 for _ in range(NCH2)]
            spev = [[0] * TCH2 for _ in range(NCH2)]
            findone = [[0] * TCH2 for _ in range(NCH2)]
            stdone = [[0] * TCH2 for _ in range(NCH2)]
            copyd = [[0] * FS for _ in range(NCH2)]
            relsq = [[0] * FS for _ in range(NCH2)]
            ycpd = [0] * NCH2

            import os as _os
            NCH2_RUN = int(_os.environ.get('NCH2_RUN', NCH2))
            LASTP = NCH2_RUN - 1

            def emit_preA1(Y):
                @blk.sync
                def _(e, Y=Y):
                    if Y >= 1:
                        e.wait_ge(vv, xmd[Y - 1][TCH2 - 1])
                    for tau in range(TCH2):
                        x2ld[Y] = bump(e.dma_start(
                            out=x2b[:, tau * D:(tau + 1) * D],
                            in_=xr[TCH2 * Y + tau]), sq, 16)

                for tau in range(1):
                    g = TCH2 * Y + tau

                    @blk.tensor
                    def _(e, Y=Y, tau=tau, g=g):
                        if Y >= 1:
                            e.wait_ge(vv, x2dev[Y - 1][TCH2 - 1])
                        if Y >= 3:
                            e.wait_ge(vv, spev[Y - 3][TCH2 - 1])
                        for n in range(2):
                            ps = psy1[:, n * 512:(n + 1) * 512]
                            bump(e.matmul(ps, bselb[:],
                                          gdsa[0:2, n * 512:(n + 1) * 512],
                                          start=True, stop=False), tv, 1)
                            bump(e.matmul(psy1[0:64, n * 512:(n + 1) * 512],
                                          hallT[:, g * 128:g * 128 + 64],
                                          ctg[:, n * 512:(n + 1) * 512],
                                          start=False, stop=False,
                                          skip_group_check=True), tv, 1)
                            y1mm[Y][tau] = bump(e.matmul(
                                psy1[64:128, n * 512:(n + 1) * 512],
                                hallT[:, g * 128 + 64:(g + 1) * 128],
                                ctg[:, D + n * 512:D + (n + 1) * 512],
                                start=False, stop=True,
                                skip_group_check=True), tv, 1)

            def emit_preA2(Y):
                cslx = Y % 3
                xmsums = [0] * TCH2

                for tau in range(TCH2):
                    g = TCH2 * Y + tau
                    XSL = slice((cslx * TCH2 + tau) * D,
                                (cslx * TCH2 + tau + 1) * D)

                    if tau > 0:
                        @blk.tensor
                        def _(e, Y=Y, tau=tau, g=g):
                            e.wait_ge(vv, xmd[Y][tau - 1])
                            for n in range(2):
                                ps = psy1[:, n * 512:(n + 1) * 512]
                                bump(e.matmul(ps, bselb[:],
                                              gdsa[0:2, n * 512:(n + 1) * 512],
                                              start=True, stop=False), tv, 1)
                                bump(e.matmul(
                                    psy1[0:64, n * 512:(n + 1) * 512],
                                    hallT[:, g * 128:g * 128 + 64],
                                    ctg[:, n * 512:(n + 1) * 512],
                                    start=False, stop=False,
                                    skip_group_check=True), tv, 1)
                                y1mm[Y][tau] = bump(e.matmul(
                                    psy1[64:128, n * 512:(n + 1) * 512],
                                    hallT[:, g * 128 + 64:(g + 1) * 128],
                                    ctg[:, D + n * 512:D + (n + 1) * 512],
                                    start=False, stop=True,
                                    skip_group_check=True), tv, 1)

                    @blk.vector
                    def _(e, Y=Y, tau=tau, g=g):
                        e.wait_ge(sq, x2ld[Y] - 16 * (TCH2 - 1 - tau))
                        c = chain(e, vv)
                        c(lambda: e.tensor_tensor(
                            xn2b[:], x2b[:, tau * D:(tau + 1) * D],
                            st_r[:, g:g + 1].broadcast_to([128, D]), op=OP.mult))
                        c(lambda: e.tensor_tensor(
                            xn2b[:], xn2b[:],
                            st_mr[:, g:g + 1].broadcast_to([128, D]),
                            op=OP.add))

                    @blk.vector
                    def _(e, Y=Y, tau=tau, g=g, XSL=XSL):
                        xt = x2b[:, tau * D:(tau + 1) * D]
                        xm = xmid[:, XSL]
                        if Y >= 3 and tau == 0:
                            e.wait_ge(so, stdone[Y - 3][TCH2 - 1])
                        e.drain()
                        c = chain(e, vv)
                        c(lambda: e.tensor_mul(t2a[:], xn2b[:], gdca_t[:]))
                        c(lambda: e.tensor_add(t2a[:], t2a[:], xt))
                        e.wait_ge(tv, y1mm[Y][tau])
                        xmd[Y][tau] = c(lambda: e.tensor_add(xm, t2a[:],
                                                             psy1[:]),
                                        snap=True)
                        c(lambda: e.reduce_sum(st2_m[:, g:g + 1], xm,
                                               axis=AX.X), snap=True)
                    xmsums[tau] = val(vv)

                for tau in range(TCH2):
                    g = TCH2 * Y + tau
                    XSL = slice((cslx * TCH2 + tau) * D,
                                (cslx * TCH2 + tau + 1) * D)

                    @blk.vector
                    def _(e, g=g, XSL=XSL):
                        c = chain(e, vv)
                        c(lambda: e.tensor_mul(t2a[:], xmid[:, XSL],
                                               xmid[:, XSL]))
                        c(lambda: e.reduce_sum(st2_r[:, g:g + 1], t2a[:],
                                               axis=AX.X))
                        c(lambda: e.tensor_scalar_mul(
                            st2_m[:, g:g + 1], st2_m[:, g:g + 1], 1.0 / D))
                        c(lambda: e.tensor_scalar_mul(
                            st2_r[:, g:g + 1], st2_r[:, g:g + 1], 1.0 / D))
                        c(lambda: e.tensor_mul(st2_mr[:, g:g + 1],
                                               st2_m[:, g:g + 1],
                                               st2_m[:, g:g + 1]))
                        c(lambda: e.tensor_sub(st2_r[:, g:g + 1],
                                               st2_r[:, g:g + 1],
                                               st2_mr[:, g:g + 1]))
                        c(lambda: e.tensor_scalar_add(
                            st2_r[:, g:g + 1], st2_r[:, g:g + 1], 1e-5),
                          snap=True)
                    var2_d = val(vv)

                    @blk.scalar
                    def _(e, g=g, var2_d=var2_d):
                        e.wait_ge(vv, var2_d)
                        e.activation(st2_r[:, g:g + 1], st2_r[:, g:g + 1],
                                     AF.Ln)
                        e.drain()
                        bump(e.activation(st2_r[:, g:g + 1], st2_r[:, g:g + 1],
                                          AF.Exp, scale=cvals[:, 0:1]), ss, 1)
                    sd2_d = val(ss)

                    @blk.vector
                    def _(e, g=g, sd2_d=sd2_d):
                        e.wait_ge(ss, sd2_d)
                        bump(e.scalar_tensor_tensor(
                            st2_mr[:, g:g + 1], st2_m[:, g:g + 1], -1.0,
                            st2_r[:, g:g + 1], op0=OP.mult, op1=OP.mult), vv, 1)
                    r2_d = val(vv)

                    @blk.vector
                    def _(e, Y=Y, g=g, tau=tau, r2_d=r2_d, XSL=XSL):
                        e.wait_ge(vv, r2_d)
                        c = chain(e, vv)
                        c(lambda: e.tensor_tensor(
                            xn2b[:], xmid[:, XSL],
                            st2_r[:, g:g + 1].broadcast_to([128, D]),
                            op=OP.mult))
                        c(lambda: e.tensor_tensor(
                            xn2b[:], xn2b[:],
                            st2_mr[:, g:g + 1].broadcast_to([128, D]),
                            op=OP.add))
                        ht = h2t[:, tau * D:(tau + 1) * D]
                        c(lambda: e.tensor_mul(ht, xn2b[:], cb1_bc[:]))
                        h2dn[Y][tau] = c(lambda: e.tensor_add(ht, ht,
                                                              sb_bc[:]),
                                         snap=True)

            def emit_preB(Y):
                csld = Y % 2
                for tau in range(TCH2):
                    @blk.tensor
                    def _(e, Y=Y, tau=tau):
                        e.wait_ge(vv, h2dn[Y][tau])
                        e.wait_ge(vv, xmd[Y][TCH2 - 1])
                        if tau > 0:
                            e.wait_ge(vv, x2dev[Y][tau - 1])
                        for sdx in range(DS):
                            h2tr[Y][tau] = bump(e.matmul(
                                pstr2[:, sdx * 128:(sdx + 1) * 128],
                                h2t[:, tau * D + sdx * 128:
                                    tau * D + (sdx + 1) * 128],
                                i128[:], start=True, stop=True), tv, 1)

                    @blk.vector
                    def _(e, Y=Y, tau=tau, csld=csld):
                        e.wait_ge(tv, h2tr[Y][tau])
                        if Y >= 2 and tau == 0:
                            e.wait_ge(tv, fcmm[Y - 2][FS - 1])
                        x2dev[Y][tau] = bump(e.tensor_copy(
                            x2d[:, csld * DS * NTK:
                                (csld + 1) * DS * NTK].rearrange(
                                "p (s t) -> p s t",
                                s=DS)[:, :, tau * 128:(tau + 1) * 128],
                            pstr2[:].rearrange("p (s t) -> p s t", s=DS)),
                            vv, 1)

            def emit_ffn(ch):
                csld = ch % 2
                for f in range(FS):
                    fsl = f % 2

                    @blk.sync
                    def _(e, ch=ch, f=f, fsl=fsl):
                        if ch == 0 and f == 0:
                            e.wait_ge(gq, wpr_done[0])
                        if f >= 2:
                            e.wait_ge(tv, pjmm[ch][f - 2])
                        elif ch > 0:
                            e.wait_ge(tv, pjmm[ch - 1][FS - 2 + f])
                        wld[ch][f] = bump(e.dma_start(
                            out=wprb[:, fsl * D:(fsl + 1) * D],
                            in_=wprB_d[f * 128:(f + 1) * 128, :]), sq, 16)

                    @blk.tensor
                    def _(e, ch=ch, f=f, csld=csld):
                        if f == 0:
                            e.wait_ge(vv, x2dev[ch][TCH2 - 1])
                            e.wait_ge(gq, wfc_done[0])
                        if f >= 2:
                            e.wait_ge(gs, relsq[ch][f - 2])
                        elif ch > 0:
                            e.wait_ge(gs, relsq[ch - 1][FS - 2 + f])
                        ps = psfc[:, (f % 2) * 256:(f % 2) * 256 + 256]
                        for k in range(DS):
                            fcmm[ch][f] = bump(e.matmul(
                                ps, wfcsb[:, k * 4 * D + f * 128:
                                          k * 4 * D + (f + 1) * 128],
                                x2d[:, csld * DS * NTK + k * NTK:
                                    csld * DS * NTK + (k + 1) * NTK],
                                start=(k == 0), stop=(k == DS - 1)), tv, 1)

                    @blk.vector
                    def _(e, ch=ch, f=f):
                        e.wait_ge(tv, fcmm[ch][f])
                        if f >= 8:
                            e.wait_ge(tv, pjmm[ch][f - 8])
                        elif ch > 0:
                            e.wait_ge(tv, pjmm[ch - 1][FS - 8 + f])
                        ps = psfc[:, (f % 2) * 256:(f % 2) * 256 + 256]
                        rb = relb[:, (f % 2) * NTK:(f % 2 + 1) * NTK]
                        e.tensor_single_scalar(rb, ps, 0.0, op=OP.max)
                        e.drain()
                        relsq[ch][f] = bump(e.tensor_mul(
                            actb[:, (f % 8) * NTK:(f % 8 + 1) * NTK],
                            rb, rb), gs, 1)
                        copyd[ch][f] = relsq[ch][f]

                    @blk.tensor
                    def _(e, ch=ch, f=f, fsl=fsl):
                        e.wait_ge(gs, relsq[ch][f])
                        e.wait_ge(sq, wld[ch][f])
                        if f == 0 and ch > 0:
                            e.wait_ge(vv, ycpd[ch - 1])
                        for dd in range(DS):
                            pjmm[ch][f] = bump(e.matmul(
                                pspj[:, dd * NTK:(dd + 1) * NTK],
                                wprb[:, fsl * D + dd * 128:
                                     fsl * D + (dd + 1) * 128],
                                actb[:, (f % 8) * NTK:(f % 8 + 1) * NTK],
                                start=(f == 0), stop=(f == FS - 1)), tv, 1)

            def emit_ycp(ch):
                @blk.vector
                def _(e, ch=ch):
                    e.wait_ge(tv, pjmm[ch][FS - 1])
                    for dd in range(DS):
                        e.tensor_copy(ycp[:, dd * NTK:(dd + 1) * NTK],
                                      pspj[:, dd * NTK:(dd + 1) * NTK])
                    ycpd[ch] = bump(e.drain(), vv, 1)

            def emit_spike(ch):
                @blk.vector
                def _(e, ch=ch):
                    if ch >= 2:
                        e.wait_ge(tv, sptr[ch - 2][TCH2 - 1])
                    sb = (ch % 2) * DS * NTK

                    def SL(t, dd):
                        return t[:, dd * NTK:(dd + 1) * NTK]

                    def SK(dd):
                        return spka[:, sb + dd * NTK:sb + (dd + 1) * NTK]

                    def stage(mk):
                        for dd in range(DS):
                            mk(dd)
                        e.drain()

                    stage(lambda dd: e.tensor_single_scalar(
                        SL(sp1, dd), SL(ycp, dd), 1.0, op=OP.is_gt))
                    stage(lambda dd: e.scalar_tensor_tensor(
                        SL(memb, dd), SL(ycp, dd), 1.9, SL(sp1, dd),
                        op0=OP.mult, op1=OP.subtract))
                    stage(lambda dd: e.tensor_single_scalar(
                        SL(sp2, dd), SL(memb, dd), 1.0, op=OP.is_gt))
                    stage(lambda dd: e.tensor_add(
                        SK(dd), SL(sp1, dd), SL(sp2, dd)))
                    seq = [(memb, memc, sp2, sp1), (memc, memb, sp1, sp2),
                           (memb, memc, sp2, sp1)]
                    for (cur, nxt, spc, spn) in seq:
                        stage(lambda dd, cur=cur, nxt=nxt: e.scalar_tensor_tensor(
                            SL(nxt, dd), SL(cur, dd), 0.9, SL(ycp, dd),
                            op0=OP.mult, op1=OP.add))
                        stage(lambda dd, nxt=nxt, spc=spc: e.tensor_sub(
                            SL(nxt, dd), SL(nxt, dd), SL(spc, dd)))
                        stage(lambda dd, nxt=nxt, spn=spn: e.tensor_single_scalar(
                            SL(spn, dd), SL(nxt, dd), 1.0, op=OP.is_gt))
                        stage(lambda dd, spn=spn: e.tensor_add(
                            SK(dd), SK(dd), SL(spn, dd)))
                    for b in range(NB):
                        stage(lambda dd, b=b: e.tensor_scalar_mul(
                            SK(dd).rearrange(
                                "p (t bb cc) -> p t bb cc",
                                t=TCH2, bb=2)[:, :, b, :],
                            SK(dd).rearrange(
                                "p (t bb cc) -> p t bb cc",
                                t=TCH2, bb=2)[:, :, b, :],
                            gbsT[:, dd * 2 + b:dd * 2 + b + 1]))
                    bump(e.drain(), vv, 1)
                spdone[ch] = val(vv)

            def emit_tail(X):
                cslx = X % 3
                sb = (X % 2) * DS * NTK
                for tau in range(TCH2):
                    @blk.tensor
                    def _(e, X=X, tau=tau, sb=sb):
                        e.wait_ge(vv, spdone[X])
                        if tau > 0:
                            e.wait_ge(vv, spev[X][tau - 1])
                        else:
                            if X + 2 <= LASTP:
                                e.wait_ge(vv, x2dev[X + 2][TCH2 - 1])
                            elif X >= 1:
                                e.wait_ge(vv, spev[X - 1][TCH2 - 1])
                            else:
                                e.wait_ge(vv, x2dev[LASTP][TCH2 - 1])
                        for dd in range(DS):
                            sptr[X][tau] = bump(e.matmul(
                                pstr2[:, dd * 128:(dd + 1) * 128],
                                spka[:, sb + dd * NTK + tau * 128:
                                     sb + dd * NTK + (tau + 1) * 128],
                                i128[:], start=True, stop=True), tv, 1)

                    @blk.vector
                    def _(e, X=X, tau=tau):
                        e.wait_ge(tv, sptr[X][tau])
                        spev[X][tau] = bump(e.tensor_copy(
                            spkt[:, tau * D:(tau + 1) * D], pstr2[:]), vv, 1)

                for tau in range(TCH2):
                    g = TCH2 * X + tau
                    XSL = slice((cslx * TCH2 + tau) * D,
                                (cslx * TCH2 + tau + 1) * D)

                    @blk.vector
                    def _(e, X=X, tau=tau, XSL=XSL):
                        e.wait_ge(vv, spev[X][tau])
                        xm = xmid[:, XSL]
                        findone[X][tau] = bump(e.tensor_add(
                            xm, spkt[:, tau * D:(tau + 1) * D], xm), vv, 1)

                    @blk.scalar
                    def _(e, X=X, tau=tau, g=g, XSL=XSL):
                        e.wait_ge(vv, findone[X][tau])
                        stdone[X][tau] = bump(e.dma_start(
                            out=orr[g], in_=xmid[:, XSL]), so, 16)

            def emit_dbg(ch):
                if ch == 0:
                    @blk.gpsimd
                    def _(e):
                        e.wait_ge(vv, xmd[0][TCH2 - 1])
                        bump(e.dma_start(out=dbg_xmid,
                                         in_=xmid[:, 0:TCH2 * D]), gg, 16)

            if NCH2_RUN > 0:
                emit_preA1(0)
                emit_preA2(0)
                emit_preB(0)
            DBGSTOP = bool(_os.environ.get('DBGSTOP'))
            for ch in range(NCH2_RUN):
                if ch >= 2:
                    emit_tail(ch - 2)
                if ch + 1 <= LASTP:
                    emit_preA1(ch + 1)
                if not DBGSTOP:
                    emit_ffn(ch)
                if ch + 1 <= LASTP:
                    emit_preA2(ch + 1)
                    emit_preB(ch + 1)
                if not DBGSTOP:
                    emit_ycp(ch)
                    emit_spike(ch)
                if dbg:
                    emit_dbg(ch)
            if not DBGSTOP:
                if NCH2_RUN >= 2:
                    emit_tail(NCH2_RUN - 2)
                if NCH2_RUN >= 1:
                    emit_tail(NCH2_RUN - 1)

            @blk.sync
            def _(e):
                e.wait_ge(so, val(so))

        eC.close()

    return nc


def _make_consts():
    rows, cols = np.triu_indices(N16, 1)
    mc = np.zeros((P120, 256), np.float32)
    for k in range(P120):
        mc[k, 16 * rows[k] + cols[k]] = 1.0
        mc[k, 16 * cols[k] + rows[k]] = -1.0
    i128f = np.eye(128, dtype=np.float32)
    i16 = np.eye(16, dtype=np.float32)
    itile = np.tile(np.eye(16, dtype=np.float32).reshape(1, 256), (128, 1))
    knl = np.zeros((4, 128), np.float32)
    knl[0] = 1.0
    bsel = np.zeros((2, 128), np.float32)
    bsel[0, 0:64] = 1.0
    bsel[1, 64:128] = 1.0
    esel = np.zeros((2, 32), np.float32)
    esel[0, 0:16] = 1.0
    esel[1, 16:32] = 1.0
    return mc, i128f, i16, np.ascontiguousarray(itile), knl, bsel, esel


def _u0_global(np_dtype):
    import jax
    import jax.numpy as jnp
    cpu = jax.devices("cpu")[0]
    with jax.default_device(cpu):
        key = jax.random.key(1, impl="threefry2x32")
        if np_dtype == np.float64:
            from jax.experimental import enable_x64
            with enable_x64():
                key = jax.random.key(1, impl="threefry2x32")
                u = jax.random.normal(key, (16, T, N16, 1), dtype=jnp.float64)
                n = jnp.maximum(jnp.linalg.norm(u, axis=2, keepdims=True),
                                1e-12)
                return np.asarray(u / n)[..., 0]
        u = jax.random.normal(key, (16, T, N16, 1), dtype=jnp.float32)
        n = jnp.maximum(jnp.linalg.norm(u, axis=2, keepdims=True), 1e-12)
        return np.asarray(u / n)[..., 0]


def scan_layout(a):
    k = a.shape[-1]
    v = a.reshape(2, CCH, L, k)
    v = np.moveaxis(v, 2, 0).reshape(L, 128, k)
    v = np.moveaxis(v, 0, 1).reshape(128, L * k)
    return np.ascontiguousarray(v)


def kernel(L0_attn_u, L0_attn_W_proj_w, L0_attn_W_proj_b, L0_attn_B_w,
           L0_attn_C_w, L0_attn_D, L0_ffn_W_fc, L0_ffn_W_proj, L0_c, L0_wmod,
           _dbg=False):
    import ml_dtypes
    xf = np.asarray(L0_attn_u)
    mc, i128f, i16, itile, knl, bsel, esel = _make_consts()
    u0g = _u0_global(xf.dtype.type)

    nc = build_nc(dbg=_dbg)

    wpT = np.ascontiguousarray(np.asarray(L0_attn_W_proj_w).T)
    wpb = np.ascontiguousarray(np.asarray(L0_attn_W_proj_b).reshape(P120, 1))
    bwT = np.ascontiguousarray(np.asarray(L0_attn_B_w).T)
    cT = np.ascontiguousarray(np.asarray(L0_attn_C_w).T)
    dv = np.ascontiguousarray(np.asarray(L0_attn_D).reshape(1, D))
    wfcT = np.ascontiguousarray(np.asarray(L0_ffn_W_fc).T)
    wprT = np.ascontiguousarray(np.asarray(L0_ffn_W_proj).T)
    wmod = np.ascontiguousarray(np.asarray(L0_wmod))
    cmod_full = np.asarray(L0_c)

    in_maps = []
    for c in range(8):
        xs = np.ascontiguousarray(xf[2 * c:2 * c + 2])
        u0c = scan_layout(
            np.ascontiguousarray(u0g[2 * c:2 * c + 2]).astype(xs.dtype))
        in_maps.append({
            "x": xs,
            "wpT": wpT, "wpb": wpb, "bwT": bwT, "cT": cT, "dvec": dv,
            "wfcT": wfcT, "wprT": wprT,
            "cmod": np.ascontiguousarray(cmod_full[2 * c:2 * c + 2]),
            "wmod": wmod,
            "mc": mc, "i128b": i128f.astype(ml_dtypes.bfloat16),
            "i128f": i128f, "i16": i16, "itile": itile,
            "cvals": np.tile(np.array([[-0.5, 0.5]], np.float32), (128, 1)),
            "knl": knl, "bsel": bsel,
            "bselb": bsel.astype(ml_dtypes.bfloat16), "esel": esel,
            "u0": u0c,
        })

    res = run_bass_kernel_spmd(nc, in_maps, core_ids=list(range(8)))
    outs = [np.asarray(r["out"]) for r in res.results]
    if _dbg:
        kernel._dbg_results = res.results
    return np.concatenate(outs, axis=0)
